# version 07
# speedup vs baseline: 1.4745x; 1.0098x over previous
import contextlib
import numpy as np
import ml_dtypes

import concourse.bass as bass
import concourse.mybir as mybir
from concourse.bass_utils import run_bass_kernel_spmd

F32 = mybir.dt.float32
BF16 = mybir.dt.bfloat16
I32 = mybir.dt.int32
AF = mybir.ActivationFunctionType
ALU = mybir.AluOpType
AX = mybir.AxisListType

NCORE = 8
D = 2048
S = 8192
HPC = 2
TOK = S // NCORE
NTC = 16
QB = 512
NQB = S // QB
GRP = [3] * 21 + [1]
NG = len(GRP)
SHIFT = -16.0
ISQ = 1.0 / float(np.sqrt(128.0))
EPS = 1e-5

DEV = False


class Ctr:
    def __init__(self, sem):
        self.sem = sem
        self.n = 0


def build_nc(dev=False):
    nc = bass.Bass("TRN2", target_bir_lowering=False, enable_partition_id=True)

    xfull = nc.dram_tensor("xfull", [D, S], F32, kind="ExternalInput").ap()
    xsh_d = nc.dram_tensor("xsh", [D, TOK], F32, kind="ExternalInput").ap()
    wq_d = nc.dram_tensor("wq", [D, 768], F32, kind="ExternalInput").ap()
    bq_d = nc.dram_tensor("bq", [128, 6], F32, kind="ExternalInput").ap()
    wmod_d = nc.dram_tensor("wmod", [16, 12, 128, 128], F32, kind="ExternalInput").ap()
    cvec_d = nc.dram_tensor("cvec", [128, 16], F32, kind="ExternalInput").ap()
    ow_d = nc.dram_tensor("ow", [16, 128, D], F32, kind="ExternalInput").ap()
    outb_d = nc.dram_tensor("outb", [128, 16], F32, kind="ExternalInput").ap()
    rw_d = nc.dram_tensor("rw", [16, 128, 8], F32, kind="ExternalInput").ap()
    wd_d = nc.dram_tensor("wd", [16, 128, 128], F32, kind="ExternalInput").ap()
    wu_d = nc.dram_tensor("wu", [128, D], F32, kind="ExternalInput").ap()
    idbf_d = nc.dram_tensor("idbf", [128, 128], BF16, kind="ExternalInput").ap()
    idf_d = nc.dram_tensor("idf", [128, 128], F32, kind="ExternalInput").ap()
    emat_d = nc.dram_tensor("emat", [8, 128], BF16, kind="ExternalInput").ap()

    out_d = nc.dram_tensor("out", [D, TOK], F32, kind="ExternalOutput").ap()

    mod_in = nc.dram_tensor("mod_in", [1536], F32)
    mod_out = nc.dram_tensor("mod_out", [12288], F32)
    st_in = nc.dram_tensor("st_in", [2048], F32)
    st_out = nc.dram_tensor("st_out", [16384], F32)
    q_dram = nc.dram_tensor("q_dram", [HPC, 128, S], BF16)
    qn_dram = nc.dram_tensor("qn_dram", [HPC, S], F32)
    ctx_in = nc.dram_tensor("ctx_in", [HPC * 128, S], BF16)
    ctx_outA = nc.dram_tensor("ctx_outA", [NCORE * 128, S], BF16)
    ctx_outB = nc.dram_tensor("ctx_outB", [NCORE * 128, S], BF16)
    x1_dram = nc.dram_tensor("x1_dram", [16, 128, TOK], F32)

    dbg = {}
    if dev:
        dbg["mod"] = nc.dram_tensor("dbg_mod", [128, 96], F32, kind="ExternalOutput").ap()
        dbg["modin"] = nc.dram_tensor("dbg_modin", [12, 128], F32, kind="ExternalOutput").ap()
        dbg["q"] = nc.dram_tensor("dbg_q", [2, 128, 512], F32, kind="ExternalOutput").ap()
        dbg["zacc"] = nc.dram_tensor("dbg_zacc", [128, 512], F32, kind="ExternalOutput").ap()
        dbg["cf"] = nc.dram_tensor("dbg_cf", [128, 16 * TOK], F32, kind="ExternalOutput").ap()
        dbg["zinvb"] = nc.dram_tensor("dbg_zinvb", [128, 512], F32, kind="ExternalOutput").ap()
        dbg["pp"] = nc.dram_tensor("dbg_pp", [128, 1536], F32, kind="ExternalOutput").ap()
        dbg["r64"] = nc.dram_tensor("dbg_r64", [64, 128], F32, kind="ExternalOutput").ap()
        dbg["k"] = nc.dram_tensor("dbg_k", [128, 512], F32, kind="ExternalOutput").ap()
        dbg["kf"] = nc.dram_tensor("dbg_kf", [128, 2 * S], F32, kind="ExternalOutput").ap()
        dbg["qf"] = nc.dram_tensor("dbg_qf", [2, 128, S], F32, kind="ExternalOutput").ap()
        dbg["v"] = nc.dram_tensor("dbg_v", [128, 512], F32, kind="ExternalOutput").ap()
        dbg["ctx"] = nc.dram_tensor("dbg_ctx", [256, S], F32, kind="ExternalOutput").ap()
        dbg["x1"] = nc.dram_tensor("dbg_x1", [128, TOK], F32, kind="ExternalOutput").ap()
        dbg["gate"] = nc.dram_tensor("dbg_gate", [128, 64], F32, kind="ExternalOutput").ap()
        dbg["x1f"] = nc.dram_tensor("dbg_x1f", [16, 128, TOK], F32, kind="ExternalOutput").ap()
        dbg["a2"] = nc.dram_tensor("dbg_a2", [128, TOK], F32, kind="ExternalOutput").ap()
        dbg["het"] = nc.dram_tensor("dbg_het", [128, TOK], F32, kind="ExternalOutput").ap()
        dbg["r2b"] = nc.dram_tensor("dbg_r2b", [128, TOK], F32, kind="ExternalOutput").ap()
        dbg["gbp"] = nc.dram_tensor("dbg_gbp", [128, 16], F32, kind="ExternalOutput").ap()
        dbg["usb"] = nc.dram_tensor("dbg_usb", [128, D], F32, kind="ExternalOutput").ap()

    es = contextlib.ExitStack()
    with es:
        sb = lambda nm, sh, dt: es.enter_context(nc.sbuf_tensor(nm, sh, dt))
        psf = lambda nm, sh: es.enter_context(nc.psum_tensor(nm, sh, F32))

        RK = sb("RK", [128, 16384], BF16)
        RV = sb("RV", [128, 16384], BF16)
        RW = sb("RW", [128, 36864], BF16)
        RP = sb("RP", [128, 5632], BF16)
        SC = sb("SC", [128, 4864], F32)
        silc = sb("silc", [128, 16], F32)
        modv = sb("modv", [128, 96], F32)
        modbf = sb("modbf", [128, 96], BF16)
        ca1 = sb("ca1", [128, 16], F32)
        ca1s = sb("ca1s", [128, 16], F32)
        cb1 = sb("cb1", [128, 16], F32)
        ca1bf = sb("ca1bf", [128, 16], BF16)
        ca1sbf = sb("ca1sbf", [128, 16], BF16)
        cb1bf = sb("cb1bf", [128, 16], BF16)
        gap = sb("gap", [128, 16], F32)
        gbp = sb("gbp", [128, 16], F32)
        gab = sb("gab", [128, 16], F32)
        tmpa = sb("tmpa", [128, 16], F32)
        tmpb = sb("tmpb", [128, 16], F32)
        bqkv = sb("bqkv", [128, 6], F32)
        wsrF = sb("wsrF", [128, 768], BF16)
        r_T = sb("r_T", [128, 64], F32)
        r64 = sb("r64", [64, 128], F32)
        m96 = sb("m96", [96, 128], F32)
        idbf = sb("idbf_s", [128, 128], BF16)
        idf = sb("idf_s", [128, 128], F32)
        emat = sb("emat_s", [8, 128], BF16)
        onec_bf = sb("onec_bf", [128, 1], BF16)
        onerow_bf = sb("onerow_bf", [1, 128], BF16)
        e0 = sb("e0", [128, 128], BF16)
        mkneg = sb("mkneg", [1, 2], F32)
        ones128 = sb("ones128", [128, 128], F32)
        c_one = sb("c_one", [128, 1], F32)
        c_zero = sb("c_zero", [128, 1], F32)
        c_shift = sb("c_shift", [128, 1], F32)
        strow = sb("strow", [1, 2048], F32)
        qnst = strow[0:1, 0:1024]
        qnrow = strow[0:1, 1024:2048]
        st4 = sb("st4", [1, 1024], F32)
        qld = sb("qld", [128, 2 * QB], BF16)
        qst = sb("qst", [128, 2 * 512], BF16)
        cstg = sb("cstg", [128, 2 * QB], BF16)
        Wd_sb = sb("Wd_sb", [128, 2048], BF16)
        U_sb = sb("U_sb", [128, 2048], BF16)
        rw_sb = sb("rw_sb", [128, 128], BF16)
        bD = sb("bD", [128, 1], F32)
        bR = sb("bR", [8, 1], F32)
        wsd = sb("wsd", [1, 128], BF16)
        wsrr = sb("wsrr", [1, 8], BF16)
        nmu2 = sb("nmu2", [1, TOK], BF16)
        r2row = sb("r2row", [1, TOK], F32)
        r2_T = sb("r2_T", [128, 8], F32)

        SA = psf("SA", [128, 1536])
        SB2 = psf("SB2", [128, 1536])
        PC = psf("PC", [128, 512])
        PU = psf("PU", [128, 512])

        xb = [RW[:, 0:8192], RW[:, 8192:16384]]
        Wsb = RW[:, 16384:28672]
        negmu = RW[0:1, 28672:36864]
        OW = RW[:, 0:32768]
        Ksb = RK
        x1b = RK
        vraw = RV
        ctxf = RV
        Pp = [RP[:, 0:1536], RP[:, 1536:3072]]
        ztree = [RP[:, 3072 + 512 * k:3584 + 512 * k] for k in range(5)]
        a_sb = RP[:, 0:1024]
        a2_sb = RP[:, 1024:2048]
        xsq = [RP[:, 2048 + 512 * k:2560 + 512 * k] for k in range(2)]
        gateT = RP[0:8, 3072:4096]
        wm_st = [SC[:, 0:1536], SC[:, 1536:3072]]
        rB = [SC[:, 0:512], SC[:, 512:1024]]
        qtmp = [SC[:, 1024:1536], SC[:, 1536:2048]]
        at2 = [SC[:, 2048:2176], SC[:, 2176:2304]]
        zat = [SC[:, 2304 + 128 * k:2432 + 128 * k] for k in range(4)]
        zacc = SC[:, 0:512]
        zfin = SC[:, 512:1024]
        zinvb = SC[:, 1024:1536]
        r2B = SC[:, 0:1024]
        xch = [SC[:, 1024:2048], SC[:, 2048:3072]]
        ltok = SC[:, 3072:3136]
        lw = SC[:, 3136:3168]
        msk = SC[:, 3168:3232]
        lstg = SC[0:8, 3232:3744]
        het = [SC[:, 3840 + 512 * k:4352 + 512 * k] for k in range(2)]
        modrow = SC[0:1, 3328:4864]

        names = ["wc", "stx0", "stx1", "modp0", "modp1", "xp0", "xp1",
                 "qs0", "qs1", "qldp0", "qldp1", "cstp0", "cstp1",
                 "xshp0", "xshp1", "x1s0", "x1s1", "x1rp0", "x1rp1",
                 "outp0", "outp1",
                 "w", "x", "mod", "cc", "st", "prep", "mm", "ev", "evq", "evk",
                 "evv", "qdma", "rbt", "rbe", "sq", "sqr",
                 "qk", "exp", "avg", "tree", "ctxev", "vz1", "vz2", "vz3",
                 "tz1", "tz2", "cst", "cdma", "qldd",
                 "owd", "ctxl", "opps", "xshd", "opev", "x1d", "x1bc", "st2",
                 "st2r", "ln2", "ffa", "ffb", "ffc", "ffd", "upev", "x1r",
                 "outd", "dbgs", "misc", "nmv", "ncon"]
        C = {nm: Ctr(es.enter_context(nc.semaphore("s_" + nm))) for nm in names}

        prog = {k: [] for k in "tvsgy"}
        TE, VE, SE, GE, YE = (lambda f: prog["t"].append(f), lambda f: prog["v"].append(f),
                              lambda f: prog["s"].append(f), lambda f: prog["g"].append(f),
                              lambda f: prog["y"].append(f))

        def W(q, ctr, val=None):
            v = ctr.n if val is None else val
            q(lambda e, s=ctr.sem, v=v: e.wait_ge(s, v))

        def _init(v):
            v.memset(onec_bf[:, :], 1.0)
            v.memset(onerow_bf[:, :], 1.0)
            v.memset(e0[:, :], 0.0)
            v.memset(e0[0:1, :], 1.0)
            v.memset(wsrF[:, :], 0.0)
            v.memset(RW[:, 28672:36864], 0.0)
            v.memset(ones128[:, :], 1.0)
            v.memset(c_one[:, :], 1.0)
            v.memset(c_zero[:, :], 0.0)
            v.memset(c_shift[:, :], SHIFT)
        VE(_init)
        C["misc"].n += 1
        VE(lambda v, s=C["misc"].sem, k=C["misc"].n: v.drain().then_inc(s, 1))
        n_init = C["misc"].n

        def gd(dst, src, ctr):
            ctr.n += 16
            GE(lambda g, dst=dst, src=src, s=ctr.sem: g.dma_start(out=dst, in_=src).then_inc(s, 16))

        gd(idbf[:, :], idbf_d, C["wc"])
        gd(idf[:, :], idf_d, C["wc"])
        gd(emat[:, :], emat_d, C["wc"])
        gd(silc[:, :], cvec_d, C["wc"])
        n_w0 = C["wc"].n

        W(SE, C["wc"], n_w0)
        W(SE, C["misc"], n_init)
        C["prep"].n += 1
        SE(lambda s, sm=C["prep"].sem: s.activation(
            silc[:, :], silc[:, :], AF.Silu, bias=c_zero[:, 0:1], scale=c_one[:, 0:1]).then_inc(sm, 1))
        n_prep_sil = C["prep"].n

        for dc in range(16):
            if dc >= 2:
                W(GE, C["mm"], dc - 1)
            gd(wm_st[dc % 2].rearrange("p (m f) -> p m f", f=128),
               wmod_d[dc, :, :, :].transpose([1, 0, 2]), C["modp%d" % (dc % 2)])
        for sc_i in range(2):
            gd(xb[sc_i].rearrange("p (c f) -> p c f", f=512),
               xsh_d.rearrange("(c p) t -> p c t", p=128)[:, :, 512 * sc_i:512 * (sc_i + 1)],
               C["stx%d" % sc_i])
        gd(Wsb.rearrange("p (c f) -> p c f", f=768),
           wq_d.rearrange("(c p) f -> p c f", p=128), C["w"])
        gd(bqkv[:, :], bq_d, C["w"])
        gd(gab[:, :], outb_d, C["w"])
        gd(Wd_sb.rearrange("p (c f) -> p c f", f=128), wd_d.transpose([1, 0, 2]), C["w"])
        gd(rw_sb.rearrange("p (c f) -> p c f", f=8), rw_d.transpose([1, 0, 2]), C["w"])
        gd(U_sb[:, :], wu_d, C["w"])
        n_w1 = C["w"].n
        W(TE, C["prep"], n_prep_sil)
        for dc in range(16):
            W(TE, C["modp%d" % (dc % 2)], 16 * (dc // 2 + 1))
            def _mv(t, dc=dc, sm=C["mm"].sem):
                for part in range(3):
                    ins = t.matmul(SA[0:1, 512 * part:512 * (part + 1)],
                                   silc[:, dc:dc + 1],
                                   wm_st[dc % 2][:, 512 * part:512 * (part + 1)],
                                   start=(dc == 0), stop=(dc == 15))
                ins.then_inc(sm, 1)
            TE(_mv)
            C["mm"].n += 1
        n_mm_mod = C["mm"].n
        W(VE, C["mm"], n_mm_mod)
        C["ev"].n += 1
        VE(lambda v, sm=C["ev"].sem: v.tensor_copy(modrow[0:1, :], SA[0:1, 0:1536]).then_inc(sm, 1))
        W(YE, C["ev"], C["ev"].n)
        C["st"].n += 16
        YE(lambda y, sm=C["st"].sem: y.dma_start(
            out=mod_in[:].rearrange("(a k) -> a k", a=1), in_=modrow[0:1, :]).then_inc(sm, 16))
        W(GE, C["st"], C["st"].n)
        C["cc"].n += 1
        GE(lambda g, sm=C["cc"].sem: g.collective_compute(
            "AllGather", ALU.bypass, replica_groups=[list(range(NCORE))],
            ins=[mod_in.ap().opt()], outs=[mod_out.ap().opt()]).then_inc(sm, 1))
        W(GE, C["cc"], C["cc"].n)
        C["st"].n += 16
        GE(lambda g, sm=C["st"].sem: g.dma_start(
            out=m96[:, :], in_=mod_out[:].rearrange("(c p) -> c p", p=128)).then_inc(sm, 16))
        W(TE, C["st"], C["st"].n)
        C["mm"].n += 1
        TE(lambda t, sm=C["mm"].sem: t.transpose(SA[:, 128:224], m96[:, :], idf[0:96, 0:96]).then_inc(sm, 1))
        W(VE, C["mm"], C["mm"].n)
        C["ev"].n += 1
        VE(lambda v, sm=C["ev"].sem: v.tensor_copy(modv[:, :], SA[:, 128:224]).then_inc(sm, 1))

        W(VE, C["w"], n_w1)
        def _modprep(v, sm=C["prep"].sem):
            v.tensor_scalar(ca1[:, :], modv[:, 16:32], 1.0, None, ALU.add)
            v.tensor_scalar(cb1[:, :], modv[:, 64:80], 1.0, None, ALU.add)
            v.drain()
            v.tensor_scalar(ca1s[:, :], ca1[:, :], ISQ, None, ALU.mult)
            for (sl, dst) in ((slice(32, 48), gap), (slice(80, 96), gbp)):
                v.tensor_scalar(tmpa[:, :], modv[:, sl], 0.25, None, ALU.max)
                v.tensor_scalar(tmpb[:, :], modv[:, sl], -0.25, None, ALU.min)
                v.tensor_scalar(dst[:, :], modv[:, sl], 0.0, None, ALU.is_ge)
                v.drain()
                v.tensor_sub(tmpa[:, :], tmpa[:, :], tmpb[:, :])
                v.drain()
                v.tensor_mul(tmpa[:, :], dst[:, :], tmpa[:, :])
                v.drain()
                v.tensor_add(dst[:, :], tmpa[:, :], tmpb[:, :])
                v.drain()
            v.tensor_mul(gab[:, :], gap[:, :], gab[:, :])
            v.tensor_copy(modbf[:, :], modv[:, :])
            v.tensor_copy(ca1bf[:, :], ca1[:, :])
            v.tensor_copy(ca1sbf[:, :], ca1s[:, :])
            v.tensor_copy(cb1bf[:, :], cb1[:, :])
            v.drain().then_inc(sm, 1)
        C["prep"].n += 1
        VE(_modprep)
        n_prep_mod = C["prep"].n

        if dev:
            W(YE, C["prep"], n_prep_mod)
            C["dbgs"].n += 16
            YE(lambda y, sm=C["dbgs"].sem: y.dma_start(out=dbg["mod"], in_=modv[:, :]).then_inc(sm, 16))
            C["dbgs"].n += 16
            YE(lambda y, sm=C["dbgs"].sem: y.dma_start(
                out=dbg["modin"], in_=mod_in[:].rearrange("(m p) -> m p", p=128)).then_inc(sm, 16))

        W(TE, C["w"], n_w1)
        W(TE, C["prep"], n_prep_mod)
        def _bmv(t, sm=C["mm"].sem):
            for ft in range(6):
                for dc in range(16):
                    ins = t.matmul(PU[:, 16 + ft:17 + ft],
                                   Wsb[:, 768 * dc + 128 * ft:768 * dc + 128 * (ft + 1)],
                                   modbf[:, dc:dc + 1], start=(dc == 0), stop=(dc == 15))
            for ft in range(6):
                carhs = ca1sbf if ft < 2 else ca1bf
                for dc in range(16):
                    ins = t.matmul(PC[:, ft:ft + 1],
                                   Wsb[:, 768 * dc + 128 * ft:768 * dc + 128 * (ft + 1)],
                                   carhs[:, dc:dc + 1], start=(dc == 0), stop=(dc == 15))
            ins.then_inc(sm, 1)
        C["mm"].n += 1
        TE(_bmv)
        n_mm_bmv = C["mm"].n
        W(VE, C["mm"], n_mm_bmv)
        def _bprep(v, sm=C["ev"].sem):
            v.tensor_add(bqkv[:, :], bqkv[:, :], PU[:, 16:22])
            v.drain()
            v.tensor_scalar(bqkv[:, 0:2], bqkv[:, 0:2], ISQ, None, ALU.mult)
            v.drain().then_inc(sm, 1)
        C["ev"].n += 1
        VE(_bprep)
        W(VE, C["mm"], n_mm_bmv)
        C["ev"].n += 1
        VE(lambda v, sm=C["ev"].sem: v.tensor_copy(tmpb[:, 0:6], PC[:, 0:6]).then_inc(sm, 1))
        W(TE, C["ev"], C["ev"].n)
        for ft in range(6):
            C["mm"].n += 1
            TE(lambda t, ft=ft, sm=C["mm"].sem: t.transpose(
                SB2[0:1, 128 * ft:128 * (ft + 1)], tmpb[:, ft:ft + 1], idf[:, :]).then_inc(sm, 1))
        W(VE, C["mm"], C["mm"].n)
        def _wsrf(v, sm=C["ev"].sem):
            v.tensor_copy(wsrF[0:1, :], SB2[0:1, 0:768])
            v.drain().then_inc(sm, 1)
        C["ev"].n += 1
        VE(_wsrf)
        n_ev_wsr = C["ev"].n

        def _fmv(t, sm=C["mm"].sem):
            for dc in range(16):
                ins = t.matmul(PU[:, 32:33], Wd_sb[:, 128 * dc:128 * (dc + 1)],
                               modbf[:, 48 + dc:49 + dc], start=(dc == 0), stop=(dc == 15))
            for dc in range(16):
                ins = t.matmul(PU[:, 34:35], Wd_sb[:, 128 * dc:128 * (dc + 1)],
                               cb1bf[:, dc:dc + 1], start=(dc == 0), stop=(dc == 15))
            for dc in range(16):
                ins = t.matmul(PU[0:8, 36:37], rw_sb[:, 8 * dc:8 * (dc + 1)],
                               modbf[:, 48 + dc:49 + dc], start=(dc == 0), stop=(dc == 15))
            for dc in range(16):
                ins = t.matmul(PU[0:8, 38:39], rw_sb[:, 8 * dc:8 * (dc + 1)],
                               cb1bf[:, dc:dc + 1], start=(dc == 0), stop=(dc == 15))
            ins.then_inc(sm, 1)
        C["mm"].n += 1
        TE(_fmv)
        n_mm_fmv = C["mm"].n
        W(VE, C["mm"], n_mm_fmv)
        C["ev"].n += 1
        VE(lambda v, sm=C["ev"].sem: (v.tensor_copy(bD[:, :], PU[:, 32:33]),
                                      v.tensor_copy(bR[:, :], PU[0:8, 36:37]),
                                      v.drain().then_inc(sm, 1))[-1])
        C["ev"].n += 1
        VE(lambda v, sm=C["ev"].sem: (v.tensor_copy(tmpb[:, 6:7], PU[:, 34:35]),
                                      v.tensor_copy(tmpb[0:8, 7:8], PU[0:8, 38:39]),
                                      v.drain().then_inc(sm, 1))[-1])
        W(TE, C["ev"], C["ev"].n)
        C["mm"].n += 1
        TE(lambda t, sm=C["mm"].sem: t.transpose(SB2[0:1, 768:896], tmpb[:, 6:7], idf[:, :]).then_inc(sm, 1))
        C["mm"].n += 1
        TE(lambda t, sm=C["mm"].sem: t.transpose(SB2[0:1, 896:904], tmpb[0:8, 7:8], idf[0:8, 0:8]).then_inc(sm, 1))
        W(VE, C["mm"], C["mm"].n)
        def _wsd(v, sm=C["prep"].sem):
            v.tensor_copy(wsd[0:1, :], SB2[0:1, 768:896])
            v.tensor_copy(wsrr[0:1, :], SB2[0:1, 896:904])
            for dc in range(16):
                v.tensor_scalar(Wd_sb[:, 128 * dc:128 * (dc + 1)], Wd_sb[:, 128 * dc:128 * (dc + 1)],
                                cb1[:, dc:dc + 1], None, ALU.mult)
                v.tensor_scalar(rw_sb[:, 8 * dc:8 * (dc + 1)], rw_sb[:, 8 * dc:8 * (dc + 1)],
                                cb1[:, dc:dc + 1], None, ALU.mult)
            v.drain().then_inc(sm, 1)
        C["prep"].n += 1
        VE(_wsd)
        def _wscale(v, sm=C["prep"].sem):
            for dc in range(16):
                v.tensor_scalar(Wsb[:, 768 * dc:768 * dc + 256], Wsb[:, 768 * dc:768 * dc + 256],
                                ca1s[:, dc:dc + 1], None, ALU.mult)
                v.tensor_scalar(Wsb[:, 768 * dc + 256:768 * dc + 768],
                                Wsb[:, 768 * dc + 256:768 * dc + 768],
                                ca1[:, dc:dc + 1], None, ALU.mult)
            v.drain().then_inc(sm, 1)
        C["prep"].n += 1
        VE(_wscale)
        n_prep_all = C["prep"].n

        for sc_i in range(2):
            W(SE, C["stx%d" % sc_i], 16)
            def _sq(s, sc_i=sc_i, wsm=C["sqr"].sem, ism=C["sq"].sem, base=C["sq"].n):
                for dc in range(16):
                    if 16 * sc_i + dc >= 2:
                        s.wait_ge(wsm, 16 * sc_i + dc - 1)
                    s.activation(xsq[dc % 2], xb[sc_i][:, 512 * dc:512 * (dc + 1)], AF.Square,
                                 bias=c_zero[:, 0:1], scale=c_one[:, 0:1]).then_inc(ism, 1)
            SE(_sq)
            C["sq"].n += 16
            W(TE, C["stx%d" % sc_i], 16)
            if sc_i == 0:
                W(TE, C["mm"], C["mm"].n)
            else:
                W(TE, C["ln2"], 1)
            def _stmv(t, sc_i=sc_i, qsm=C["sq"].sem, rsm=C["sqr"].sem):
                for dc in range(16):
                    t.wait_ge(qsm, 16 * sc_i + dc + 1)
                    t.matmul(SA[0:1, 0:512], onec_bf[:, 0:1], xb[sc_i][:, 512 * dc:512 * (dc + 1)],
                             start=(dc == 0), stop=(dc == 15))
                    t.matmul(SA[0:1, 512:1024], onec_bf[:, 0:1], xsq[dc % 2],
                             start=(dc == 0), stop=(dc == 15)).then_inc(rsm, 1)
            TE(_stmv)
            C["sqr"].n += 16
            W(VE, C["sqr"], C["sqr"].n)
            def _stf(v, sc_i=sc_i, sm=C["ln2"].sem):
                slm = slice(512 * sc_i, 512 * (sc_i + 1))
                v.tensor_scalar(strow[0:1, slm], SA[0:1, 0:512], -1.0 / D, None, ALU.mult)
                v.tensor_scalar(st4[0:1, 0:512], SA[0:1, 512:1024], 1.0 / D, None, ALU.mult)
                v.drain()
                v.tensor_mul(st4[0:1, 512:1024], strow[0:1, slm], strow[0:1, slm])
                v.drain()
                v.tensor_sub(st4[0:1, 0:512], st4[0:1, 0:512], st4[0:1, 512:1024])
                v.drain()
                v.tensor_scalar(st4[0:1, 0:512], st4[0:1, 0:512], EPS, None, ALU.add)
                v.drain()
                v.reciprocal(st4[0:1, 512:1024], st4[0:1, 0:512])
                v.drain().then_inc(sm, 1)
            C["ln2"].n += 1
            VE(_stf)
            W(SE, C["ln2"], C["ln2"].n)
            C["ln2"].n += 1
            SE(lambda s, sm=C["ln2"].sem: s.activation(
                st4[0:1, 512:1024], st4[0:1, 512:1024], AF.Sqrt,
                bias=c_zero[0:1, 0:1], scale=c_one[0:1, 0:1]).then_inc(sm, 1))
            W(VE, C["ln2"], C["ln2"].n)
            def _newt(v, sc_i=sc_i, sm=C["ln2"].sem):
                slr = slice(1024 + 512 * sc_i, 1024 + 512 * (sc_i + 1))
                v.tensor_mul(strow[0:1, slr], st4[0:1, 512:1024], st4[0:1, 512:1024])
                v.drain()
                v.tensor_mul(strow[0:1, slr], strow[0:1, slr], st4[0:1, 0:512])
                v.drain()
                v.tensor_scalar(strow[0:1, slr], strow[0:1, slr], -0.5, 1.5, ALU.mult, ALU.add)
                v.drain()
                v.tensor_mul(strow[0:1, slr], strow[0:1, slr], st4[0:1, 512:1024])
                v.drain().then_inc(sm, 1)
            C["ln2"].n += 1
            VE(_newt)
        W(YE, C["ln2"], C["ln2"].n)
        C["st"].n += 16
        YE(lambda y, sm=C["st"].sem: y.dma_start(
            out=st_in[:].rearrange("(a k) -> a k", a=1), in_=strow[0:1, :]).then_inc(sm, 16))
        n_stats_st = C["st"].n
        W(GE, C["st"], C["st"].n)
        C["cc"].n += 1
        GE(lambda g, sm=C["cc"].sem: g.collective_compute(
            "AllGather", ALU.bypass, replica_groups=[list(range(NCORE))],
            ins=[st_in.ap().opt()], outs=[st_out.ap().opt()]).then_inc(sm, 1))
        W(GE, C["cc"], C["cc"].n)
        W(GE, C["misc"], n_init)
        C["st"].n += 16
        GE(lambda g, sm=C["st"].sem: g.dma_start(
            out=negmu.rearrange("p (r k) -> p r k", r=8),
            in_=st_out[:].rearrange("(r t k) -> t r k", r=8, t=2)[0:1, :, :]).then_inc(sm, 16))
        for rr in range(8):
            C["st"].n += 16
            GE(lambda g, rr=rr, sm=C["st"].sem: g.dma_start(
                out=r64[8 * rr:8 * (rr + 1), :],
                in_=st_out[:].rearrange("(r t j k) -> r t j k", r=8, t=2, j=8)[rr, 1, :, :]).then_inc(sm, 16))
        n_st_rows = C["st"].n
        W(TE, C["st"], n_st_rows)
        C["mm"].n += 1
        TE(lambda t, sm=C["mm"].sem: t.transpose(PU[:, 64:128], r64[:, :], idf[0:64, 0:64]).then_inc(sm, 1))
        W(VE, C["mm"], C["mm"].n)
        C["ev"].n += 1
        VE(lambda v, sm=C["ev"].sem: v.tensor_copy(r_T[:, :], PU[:, 64:128]).then_inc(sm, 1))
        n_ev_rT = C["ev"].n

        if dev:
            W(YE, C["st"], n_st_rows)
            C["dbgs"].n += 16
            YE(lambda y, sm=C["dbgs"].sem: y.dma_start(out=dbg["r64"], in_=r64[:, :]).then_inc(sm, 16))

        qk_slots = [SA[:, 0:512], SA[:, 512:1024], SA[:, 1024:1536], SB2[:, 0:512]]
        v_slots = [SB2[:, 512:1024], SB2[:, 1024:1536]]
        qsqb = [RP[:, 0:512], RP[:, 512:1024]]
        qn_st_base = C["st"].n
        for tc in range(NTC):
            if tc < 2:
                W(GE, C["sqr"], 32)
            else:
                W(GE, C["evv"], 4 * (tc - 1))
            gd(xb[tc % 2].rearrange("p (c f) -> p c f", f=512),
               xfull.rearrange("(c p) t -> p c t", p=128)[:, :, 512 * tc:512 * (tc + 1)],
               C["xp%d" % (tc % 2)])
        for tc in range(NTC):
            W(VE, C["ev"], n_ev_rT)
            def _repl(v, tc=tc, tsm=C["rbt"].sem, esm=C["rbe"].sem,
                      tb=C["rbt"].n, eb=C["rbe"].n):
                for j in range(4):
                    if 4 * tc + j >= 2:
                        v.wait_ge(esm, eb + j - 2 + 1)
                    v.tensor_scalar(at2[j % 2], ones128[:, :], r_T[:, 4 * tc + j:4 * tc + j + 1],
                                    None, ALU.mult).then_inc(tsm, 1)
            VE(_repl)
            C["rbt"].n += 4
            def _trb(t, tc=tc, tsm=C["rbt"].sem, esm=C["rbe"].sem,
                     tb=C["rbt"].n - 4, vsm=C["ev"].sem, need_ev=C["ev"].n):
                for j in range(4):
                    t.wait_ge(tsm, tb + j + 1)
                    if tc >= 1 and j == 0:
                        t.wait_ge(vsm, need_ev)
                    t.transpose(PC[:, 128 * j:128 * (j + 1)], at2[j % 2], idf[:, :]).then_inc(esm, 1)
            if tc >= 1:
                pass
            TE(_trb)
            C["rbe"].n += 4
            W(VE, C["rbe"], C["rbe"].n)
            C["ev"].n += 1
            VE(lambda v, tc=tc, sm=C["ev"].sem: v.tensor_copy(rB[tc % 2][:, :], PC[:, :]).then_inc(sm, 1))

            W(TE, C["xp%d" % (tc % 2)], 16 * (tc // 2 + 1))
            if tc == 0:
                W(TE, C["prep"], n_prep_all)
                W(TE, C["ev"], n_ev_wsr)
                W(TE, C["ln2"], 6)
            def _qk(t, tc=tc, sm=C["qk"].sem, esm=C["evk"].sem):
                for ft in range(4):
                    gidx = 4 * tc + ft
                    if gidx >= 4:
                        t.wait_ge(esm, gidx - 3)
                    slot = qk_slots[gidx % 4]
                    for dc in range(16):
                        t.matmul(slot, Wsb[:, 768 * dc + 128 * ft:768 * dc + 128 * (ft + 1)],
                                 xb[tc % 2][:, 512 * dc:512 * (dc + 1)], start=(dc == 0), stop=False)
                    t.matmul(slot, wsrF[:, 128 * ft:128 * (ft + 1)],
                             RW[:, 28672 + 512 * tc:28672 + 512 * (tc + 1)],
                             start=False, stop=True).then_inc(sm, 1)
            TE(_qk)
            C["qk"].n += 4
            def _vmm(t, tc=tc, sm=C["evv"].sem, psm=C["mm"].sem, pb=C["mm"].n):
                for tt in range(4):
                    gidx = 4 * tc + tt
                    if gidx >= 2:
                        t.wait_ge(sm, gidx - 1)
                    slot = v_slots[tt % 2]
                    for dc in range(16):
                        t.matmul(slot[:, 0:256],
                                 xb[tc % 2][:, 512 * dc + 128 * tt:512 * dc + 128 * (tt + 1)],
                                 Wsb[:, 768 * dc + 512:768 * dc + 768], start=(dc == 0), stop=False)
                    t.matmul(slot[:, 0:256],
                             RW[:, 28672 + 512 * tc + 128 * tt:28672 + 512 * tc + 128 * (tt + 1)],
                             wsrF[:, 512:768], start=False, stop=True).then_inc(psm, 1)
            TE(_vmm)
            C["mm"].n += 4
            def _evqk(v, tc=tc, qsm=C["qk"].sem, qb_=C["qk"].n - 4,
                      ksm=C["evk"].sem, qesm=C["evq"].sem,
                      dsm=(C["qs0"].sem, C["qs1"].sem), sqs=C["sq"].sem,
                      rbready=C["ev"].n):
                for ft in range(4):
                    v.wait_ge(qsm, qb_ + ft + 1)
                    slot = qk_slots[(4 * tc + ft) % 4]
                    tb = qtmp[ft % 2]
                    ins = v.tensor_mul(tb, slot, rB[tc % 2][:, :])
                    if ft < 2:
                        ins.then_inc(ksm, 1)
                        if tc >= 1:
                            v.wait_ge(dsm[ft], 16 * tc)
                            v.wait_ge(sqs, 4 * tc + 30)
                        v.tensor_scalar(qst[:, 512 * ft:512 * (ft + 1)], tb,
                                        bqkv[:, ft:ft + 1], None, ALU.add).then_inc(qesm, 1)
                    else:
                        h = ft - 2
                        v.tensor_scalar(Ksb[:, 8192 * h + 512 * tc:8192 * h + 512 * (tc + 1)],
                                        tb, bqkv[:, ft:ft + 1], None, ALU.add).then_inc(ksm, 1)
            VE(_evqk)
            C["evq"].n += 2
            C["evk"].n += 4
            def _evv(v, tc=tc, psm=C["mm"].sem, pb=C["mm"].n - 4, sm=C["evv"].sem):
                for tt in range(4):
                    v.wait_ge(psm, pb + tt + 1)
                    slot = v_slots[tt % 2]
                    kt = 4 * tc + tt
                    v.tensor_scalar(vraw[:, 256 * kt:256 * (kt + 1)], slot[:, 0:256],
                                    r_T[:, kt:kt + 1], None, ALU.mult).then_inc(sm, 1)
            VE(_evv)
            C["evv"].n += 4
            for ft in range(2):
                W(YE, C["evq"], C["evq"].n - 1 + ft)
                C["qs%d" % ft].n += 16
                YE(lambda y, tc=tc, ft=ft, sm=C["qs%d" % ft].sem: y.dma_start(
                    out=q_dram[ft, :, 512 * tc:512 * (tc + 1)],
                    in_=qst[:, 512 * ft:512 * (ft + 1)]).then_inc(sm, 16))
            for j in range(4):
                idx = 4 * tc + j
                def _nsq(s, tc=tc, j=j, idx=idx, sm=C["sq"].sem,
                         eq=C["evq"].sem, ek=C["evk"].sem, nm=C["nmv"].sem):
                    if idx >= 2:
                        s.wait_ge(nm, idx - 1)
                    if j < 2:
                        s.wait_ge(eq, 2 * tc + j + 1)
                    else:
                        s.wait_ge(ek, 4 * tc + j + 1)
                    src_ = (qst[:, 512 * j:512 * (j + 1)] if j < 2 else
                            Ksb[:, 8192 * (j - 2) + 512 * tc:8192 * (j - 2) + 512 * (tc + 1)])
                    s.activation(qsqb[idx % 2], src_, AF.Square,
                                 bias=c_zero[:, 0:1], scale=c_one[:, 0:1]).then_inc(sm, 1)
                SE(_nsq)
                C["sq"].n += 1
                def _nmv(t, idx=idx, qv=C["sq"].n, qs=C["sq"].sem, sm=C["nmv"].sem,
                         co=C["ncon"].sem, evs=C["ev"].sem):
                    t.wait_ge(qs, qv)
                    if idx == 0:
                        t.wait_ge(evs, n_ev_rT)
                    else:
                        t.wait_ge(co, idx)
                    t.matmul(PU[0:1, 0:512], onec_bf[:, 0:1], qsqb[idx % 2],
                             start=True, stop=True).then_inc(sm, 1)
                TE(_nmv)
                C["nmv"].n += 1
                if j < 2:
                    def _nsqrt(s, mv=C["nmv"].n, ms=C["nmv"].sem,
                               sm=C["ncon"].sem, sts=C["st"].sem, stv=C["st"].n):
                        s.wait_ge(ms, mv)
                        s.wait_ge(sts, stv)
                        s.activation(qnst[0:1, 0:512], PU[0:1, 0:512], AF.Sqrt,
                                     bias=c_zero[0:1, 0:1], scale=c_one[0:1, 0:1]).then_inc(sm, 1)
                    SE(_nsqrt)
                    C["ncon"].n += 1
                    W(YE, C["ncon"], C["ncon"].n)
                    C["st"].n += 16
                    YE(lambda y, tc=tc, j=j, sm=C["st"].sem: y.dma_start(
                        out=qn_dram[j, 512 * tc:512 * (tc + 1)].rearrange("(a k) -> a k", a=1),
                        in_=qnst[0:1, 0:512]).then_inc(sm, 16))
                else:
                    def _nkm(v, tc=tc, j=j, mv=C["nmv"].n, ms=C["nmv"].sem,
                             sm=C["ncon"].sem, sts=C["st"].sem, stv=n_stats_st):
                        v.wait_ge(ms, mv)
                        v.wait_ge(sts, stv)
                        v.tensor_reduce(
                            qnst[0:1, 512 + 16 * (j - 2) + tc:513 + 16 * (j - 2) + tc],
                            PU[0:1, 0:512], AX.X, ALU.max).then_inc(sm, 1)
                    VE(_nkm)
                    C["ncon"].n += 1
        n_sq_all = C["sq"].n
        n_evk_all = C["evk"].n
        n_evv_all = C["evv"].n
        n_qk_all = C["qk"].n

        if dev:
            W(GE, C["evk"], n_evk_all)
            W(GE, C["evv"], n_evv_all)
            W(GE, C["qs0"], C["qs0"].n)
            W(GE, C["qs1"], C["qs1"].n)
            C["dbgs"].n += 48
            def _dq(g, sm=C["dbgs"].sem):
                g.dma_start(out=dbg["k"], in_=Ksb[:, 0:512]).then_inc(sm, 16)
                g.dma_start(out=dbg["v"], in_=vraw[:, 0:512]).then_inc(sm, 16)
                g.dma_start(out=dbg["q"], in_=q_dram[:, :, 0:512]).then_inc(sm, 16)
                g.dma_start(out=dbg["kf"], in_=Ksb[:, :]).then_inc(sm, 16)
                g.dma_start(out=dbg["qf"], in_=q_dram[:, :, :]).then_inc(sm, 16)
            GE(_dq)
            C["dbgs"].n += 32

        W(VE, C["ncon"], 64)
        C["ln2"].n += 1
        VE(lambda v, sm=C["ln2"].sem: (
            v.tensor_reduce(qnst[0:1, 544:545], qnst[0:1, 512:528], AX.X, ALU.max),
            v.tensor_reduce(qnst[0:1, 545:546], qnst[0:1, 528:544], AX.X, ALU.max),
            v.drain().then_inc(sm, 1))[-1])
        W(SE, C["ln2"], C["ln2"].n)
        C["ln2"].n += 1
        SE(lambda s, sm=C["ln2"].sem: s.activation(
            qnst[0:1, 544:546], qnst[0:1, 544:546], AF.Sqrt,
            bias=c_zero[0:1, 0:1], scale=c_one[0:1, 0:1]).then_inc(sm, 1))
        W(VE, C["ln2"], C["ln2"].n)
        C["ln2"].n += 1
        VE(lambda v, sm=C["ln2"].sem: (
            v.tensor_scalar(mkneg[0:1, :], qnst[0:1, 544:546], -1.0, None, ALU.mult),
            v.drain().then_inc(sm, 1))[-1])
        n_mk = C["ln2"].n

        W(GE, C["evv"], n_evv_all)
        W(GE, C["evk"], n_evk_all)
        gd(OW.rearrange("p (c f) -> p c f", f=2048), ow_d.transpose([1, 0, 2]), C["owd"])
        n_owd = C["owd"].n

        kt_grp = []
        acc = 0
        for gsz in GRP:
            kt_grp.append((acc, gsz))
            acc += gsz
        rings = [SA, SB2]
        NQI = HPC * NQB

        nb_base = C["evq"].n
        stx_base = {0: C["stx0"].n, 1: C["stx1"].n}
        def emit_qld(qi):
            h, qb = divmod(qi, NQB)
            W(YE, C["qs%d" % h], 16 * (qb + 1))
            W(YE, C["st"], qn_st_base + 16 * (2 * qb + h + 1))
            if qi >= 2:
                W(YE, C["qk"], n_qk_all + (qi - 1) * NG)
                W(YE, C["evq"], nb_base + qi - 1)
            C["qldp%d" % (qi % 2)].n += 16
            YE(lambda y, h=h, qb=qb, qi=qi, sm=C["qldp%d" % (qi % 2)].sem: y.dma_start(
                out=qld[:, QB * (qi % 2):QB * (qi % 2 + 1)],
                in_=q_dram[h, :, QB * qb:QB * (qb + 1)]).then_inc(sm, 16))
            C["stx%d" % (qi % 2)].n += 16
            YE(lambda y, h=h, qb=qb, qi=qi, sm=C["stx%d" % (qi % 2)].sem: y.dma_start(
                out=qnrow[0:1, 512 * (qi % 2):512 * (qi % 2 + 1)],
                in_=qn_dram[h, 512 * qb:512 * (qb + 1)].rearrange("(a k) -> a k", a=1))
                .then_inc(sm, 16))
        def emit_negB(qi):
            h = qi // NQB
            W(VE, C["stx%d" % (qi % 2)], stx_base[qi % 2] + 16 * (qi // 2 + 1))
            W(VE, C["ln2"], n_mk)
            if qi < 2:
                W(VE, C["qs0"], C["qs0"].n)
                W(VE, C["qs1"], C["qs1"].n)
                W(VE, C["sq"], n_sq_all)
            if qi >= 2:
                W(VE, C["qk"], n_qk_all + (qi - 1) * NG)
            def _nb(v, qi=qi, h=h, sm=C["evq"].sem):
                v.tensor_scalar(qst[0:1, 512 * (qi % 2):512 * (qi % 2 + 1)],
                                qnrow[0:1, 512 * (qi % 2):512 * (qi % 2 + 1)],
                                mkneg[0:1, h:h + 1], 70.0, ALU.mult, ALU.add).then_inc(sm, 1)
            VE(_nb)
            C["evq"].n += 1
        emit_qld(0)
        emit_qld(1)

        qk0 = n_qk_all
        for qi in range(NQI):
            for gi, (ktb, gsz) in enumerate(kt_grp):
                G = qi * NG + gi
                W(SE, C["qk"], qk0 + G + 1)
                if G >= 2:
                    W(SE, C["avg"], G - 1)
                    W(SE, C["tree"], G - 1)
                if dev and G == 2:
                    W(SE, C["dbgs"], C["dbgs"].n + 16)
                C["exp"].n += 1
                SE(lambda s, G=G, gsz=gsz, sm=C["exp"].sem: s.activation(
                    Pp[G % 2][:, 0:512 * gsz], rings[G % 2][:, 0:512 * gsz], AF.Exp,
                    bias=c_zero[:, 0:1], scale=c_one[:, 0:1]).then_inc(sm, 1))

        if dev:
            W(GE, C["exp"], 1)
            C["dbgs"].n += 16
            GE(lambda g, sm=C["dbgs"].sem: g.dma_start(out=dbg["pp"], in_=Pp[0][:, 0:1536]).then_inc(sm, 16))
        def emit_ztr(qi):
            W(TE, C["vz1"], qi + 1)
            if dev and qi == 0:
                W(TE, C["dbgs"], C["dbgs"].n)
            if qi >= 1:
                W(TE, C["vz3"], qi)
            def _ztr(t, sm=C["tz1"].sem):
                for j in range(4):
                    ins = t.transpose(PU[:, 128 * j:128 * (j + 1)], zacc[:, 128 * j:128 * (j + 1)],
                                      idf[:, :])
                ins.then_inc(sm, 1)
            TE(_ztr)
            C["tz1"].n += 1
            W(TE, C["vz2"], qi + 1)
            def _ztr2(t, sm=C["tz2"].sem):
                for j in range(4):
                    ins = t.transpose(PU[:, 128 * j:128 * (j + 1)], zat[j], idf[:, :])
                ins.then_inc(sm, 1)
            TE(_ztr2)
            C["tz2"].n += 1
        for qi in range(NQI):
            h, qb = divmod(qi, NQB)
            qslot = qld[:, QB * (qi % 2):QB * (qi % 2 + 1)]
            for gi, (ktb, gsz) in enumerate(kt_grp):
                G = qi * NG + gi
                if G >= 2:
                    W(TE, C["exp"], G - 1)
                if G == 0:
                    W(TE, C["evk"], n_evk_all)
                    W(TE, C["evv"], n_evv_all)
                    W(TE, C["ev"], C["ev"].n)
                if gi == 0:
                    W(TE, C["qldp%d" % (qi % 2)], 16 * (qi // 2 + 1))
                    W(TE, C["evq"], nb_base + qi + 1)
                def _qkg(t, h=h, ktb=ktb, gsz=gsz, G=G, qi=qi, qslot=qslot, sm=C["qk"].sem):
                    nbs = qst[:, 512 * (qi % 2):512 * (qi % 2 + 1)]
                    for j in range(gsz):
                        kt = ktb + j
                        t.matmul(rings[G % 2][:, 512 * j:512 * (j + 1)],
                                 Ksb[:, 8192 * h + 128 * kt:8192 * h + 128 * (kt + 1)],
                                 qslot, start=True, stop=False)
                        ins = t.matmul(rings[G % 2][:, 512 * j:512 * (j + 1)],
                                       e0[:, :], nbs, start=False, stop=True)
                    ins.then_inc(sm, 1)
                TE(_qkg)
                C["qk"].n += 1
                if gi == 1 and qi >= 1:
                    emit_ztr(qi - 1)
                if gi >= 1:
                    Gp = G - 1
                    ktbp, gszp = kt_grp[gi - 1]
                    W(TE, C["exp"], Gp + 1)
                    if gi == 1:
                        W(TE, C["ctxev"], qi)
                    def _avg(t, h=h, ktbp=ktbp, gszp=gszp, Gp=Gp, first=(gi == 1), sm=C["avg"].sem):
                        for j in range(gszp):
                            kt = ktbp + j
                            ins = t.matmul(PC[:, :],
                                           vraw[:, 256 * kt + 128 * h:256 * kt + 128 * (h + 1)],
                                           Pp[Gp % 2][:, 512 * j:512 * (j + 1)],
                                           start=(first and j == 0), stop=False)
                        ins.then_inc(sm, 1)
                    TE(_avg)
                    C["avg"].n += 1
            Gl = qi * NG + NG - 1
            ktbp, gszp = kt_grp[NG - 1]
            W(TE, C["exp"], Gl + 1)
            def _avl(t, h=h, ktbp=ktbp, gszp=gszp, Gl=Gl, sm=C["avg"].sem):
                for j in range(gszp):
                    kt = ktbp + j
                    ins = t.matmul(PC[:, :],
                                   vraw[:, 256 * kt + 128 * h:256 * kt + 128 * (h + 1)],
                                   Pp[Gl % 2][:, 512 * j:512 * (j + 1)],
                                   start=False, stop=(j == gszp - 1))
                ins.then_inc(sm, 1)
            TE(_avl)
            C["avg"].n += 1
            if qi == NQI - 1:
                emit_ztr(qi)

        tree_sched = []
        for qi in range(NQI):
            pend = []
            free = list(range(5))
            persched = []
            for gi, (ktb, gsz) in enumerate(kt_grp):
                ops = []
                s0 = free.pop(0)
                ops.append(("grp", s0, gsz))
                pend.append([1, s0])
                while len(pend) >= 2 and pend[-1][0] == pend[-2][0]:
                    l1, sl1 = pend.pop()
                    l0, sl0 = pend.pop()
                    ops.append(("mrg", sl0, sl1))
                    free.append(sl1)
                    pend.append([l0 + 1, sl0])
                persched.append(ops)
            tree_sched.append((persched, [p[1] for p in pend]))

        emit_negB(0)
        emit_negB(1)
        h0_csts = []
        for qi in range(NQI):
            h, qb = divmod(qi, NQB)
            persched, rem = tree_sched[qi]
            for gi, (ktb, gsz) in enumerate(kt_grp):
                G = qi * NG + gi
                W(VE, C["exp"], G + 1)
                def _tree(v, G=G, gsz=gsz, ops=persched[gi], sm=C["tree"].sem):
                    ins = None
                    for op in ops:
                        if op[0] == "grp":
                            _, s0, gz = op
                            if gz >= 2:
                                ins = v.tensor_add(ztree[s0], Pp[G % 2][:, 0:512], Pp[G % 2][:, 512:1024])
                                if gz == 3:
                                    ins = v.tensor_add(ztree[s0], ztree[s0], Pp[G % 2][:, 1024:1536])
                            else:
                                ins = v.tensor_copy(ztree[s0], Pp[G % 2][:, 0:512])
                        else:
                            _, sl0, sl1 = op
                            ins = v.tensor_add(ztree[sl0], ztree[sl0], ztree[sl1])
                    ins.then_inc(sm, 1)
                VE(_tree)
                C["tree"].n += 1
            def _zfin(v, rem=rem, sm=C["vz1"].sem):
                ins = None
                for sl in rem:
                    if ins is None:
                        ins = v.tensor_copy(zacc[:, :], ztree[sl])
                    else:
                        ins = v.tensor_add(zacc[:, :], zacc[:, :], ztree[sl])
                ins.then_inc(sm, 1)
            VE(_zfin)
            C["vz1"].n += 1
            if dev and qi == 0:
                W(GE, C["vz1"], 1)
                C["dbgs"].n += 16
                GE(lambda g, sm=C["dbgs"].sem: g.dma_start(out=dbg["zacc"], in_=zacc[:, :]).then_inc(sm, 16))
            W(VE, C["tz1"], qi + 1)
            def _zred(v, sm=C["vz2"].sem):
                for j in range(4):
                    v.tensor_reduce(zfin[:, j:j + 1], PU[:, 128 * j:128 * (j + 1)], AX.X, ALU.add)
                v.drain()
                v.reciprocal(zfin[:, 0:4], zfin[:, 0:4])
                v.drain()
                for j in range(4):
                    ins = v.tensor_scalar(zat[j], ones128[:, :], zfin[:, j:j + 1], None, ALU.mult)
                ins.then_inc(sm, 1)
            VE(_zred)
            C["vz2"].n += 1
            W(VE, C["tz2"], qi + 1)
            def _cev(v, h=h, qi=qi, sm3=C["vz3"].sem, cse=C["ctxev"].sem, css=C["cst"].sem,
                     avs=C["avg"].sem, avn=(qi + 1) * NG,
                     dsm=C["cstp%d" % (qi % 2)].sem,
                     dneed=16 * (qi // 2) if qi >= 2 else 0):
                v.tensor_copy(zinvb[:, :], PU[:, 0:512]).then_inc(sm3, 1)
                v.wait_ge(avs, avn)
                v.tensor_mul(zacc[:, :], PC[:, :], zinvb[:, :]).then_inc(cse, 1)
                if dneed:
                    v.wait_ge(dsm, dneed)
                v.tensor_scalar(cstg[:, QB * (qi % 2):QB * (qi % 2 + 1)], zacc[:, :],
                                bqkv[:, 4 + h:5 + h], None, ALU.add).then_inc(css, 1)
            VE(_cev)
            C["vz3"].n += 1
            if dev and qi == 0:
                W(GE, C["vz3"], 1)
                C["dbgs"].n += 16
                GE(lambda g, sm=C["dbgs"].sem: g.dma_start(out=dbg["zinvb"], in_=zinvb[:, :]).then_inc(sm, 16))
            if dev and qi == 1:
                W(VE, C["dbgs"], C["dbgs"].n)
            C["ctxev"].n += 1
            C["cst"].n += 1
            W(YE, C["cst"], C["cst"].n)
            C["cstp%d" % (qi % 2)].n += 16
            YE(lambda y, h=h, qb=qb, qi=qi, sm=C["cstp%d" % (qi % 2)].sem: y.dma_start(
                out=ctx_in[128 * h:128 * (h + 1), QB * qb:QB * (qb + 1)],
                in_=cstg[:, QB * (qi % 2):QB * (qi % 2 + 1)]).then_inc(sm, 16))
            if qi == NQB - 1:
                h0_csts.append((C["cstp0"].n, C["cstp1"].n))
            if qi + 2 < NQI:
                emit_qld(qi + 2)
                emit_negB(qi + 2)

        if dev:
            W(GE, C["cstp0"], C["cstp0"].n)
            W(GE, C["cstp1"], C["cstp1"].n)
            C["dbgs"].n += 16
            GE(lambda g, sm=C["dbgs"].sem: g.dma_start(
                out=dbg["ctx"].rearrange("(a p) t -> a p t", p=128),
                in_=ctx_in[:, :].rearrange("(a p) t -> a p t", p=128)).then_inc(sm, 16))

        W(GE, C["cstp0"], h0_csts[0][0])
        W(GE, C["cstp1"], h0_csts[0][1])
        C["cc"].n += 1
        GE(lambda g, sm=C["cc"].sem: g.collective_compute(
            "AllGather", ALU.bypass, replica_groups=[list(range(NCORE))],
            ins=[ctx_in.ap()[0:128, :].opt()], outs=[ctx_outA.ap().opt()]).then_inc(sm, 1))
        W(GE, C["cstp0"], C["cstp0"].n)
        W(GE, C["cstp1"], C["cstp1"].n)
        C["cc"].n += 1
        GE(lambda g, sm=C["cc"].sem: g.collective_compute(
            "AllGather", ALU.bypass, replica_groups=[list(range(NCORE))],
            ins=[ctx_in.ap()[128:256, :].opt()], outs=[ctx_outB.ap().opt()]).then_inc(sm, 1))
        W(GE, C["cc"], C["cc"].n - 1)
        W(GE, C["avg"], NQI * NG)
        C["ctxl"].n += 16
        def _ctxlA(g, sm=C["ctxl"].sem):
            off = g.partition_id() * TOK
            g.dma_start(out=ctxf.rearrange("p (i h f) -> p i h f", i=8, h=2)[:, :, 0, :],
                        in_=ctx_outA.rearrange("(c p) t -> p c t", p=128)[:, :, bass.ds(off, TOK)]) \
                .then_inc(sm, 16)
        GE(_ctxlA)
        W(GE, C["cc"], C["cc"].n)
        C["ctxl"].n += 16
        def _ctxlB(g, sm=C["ctxl"].sem):
            off = g.partition_id() * TOK
            g.dma_start(out=ctxf.rearrange("p (i h f) -> p i h f", i=8, h=2)[:, :, 1, :],
                        in_=ctx_outB.rearrange("(c p) t -> p c t", p=128)[:, :, bass.ds(off, TOK)]) \
                .then_inc(sm, 16)
        GE(_ctxlB)

        if dev:
            W(GE, C["ctxl"], C["ctxl"].n)
            C["dbgs"].n += 16
            GE(lambda g, sm=C["dbgs"].sem: g.dma_start(out=dbg["cf"], in_=ctxf[:, :]).then_inc(sm, 16))
        op_slots = [SA[:, 0:512], SA[:, 512:1024], SA[:, 1024:1536], SB2[:, 0:512]]
        def emit_xsh(dc):
            if dc >= 2:
                W(YE, C["opev"], 2 * (dc - 1))
                W(YE, C["x1s%d" % (dc % 2)], 16 * (dc // 2))
            W(YE, C["ctxev"], NQI)
            C["xshp%d" % (dc % 2)].n += 16
            YE(lambda y, dc=dc, sm=C["xshp%d" % (dc % 2)].sem: y.dma_start(
                out=xch[dc % 2][:, :],
                in_=xsh_d.rearrange("(c p) t -> p c t", p=128)[:, dc, :]).then_inc(sm, 16))
        emit_xsh(0)
        emit_xsh(1)
        for dc in range(16):
            W(TE, C["ctxl"], C["ctxl"].n)
            W(TE, C["owd"], n_owd)
            if dc == 0:
                W(TE, C["vz3"], NQI)
            def _op(t, dc=dc, sm=C["opps"].sem, esm=C["opev"].sem):
                for th in range(2):
                    k = 2 * dc + th
                    if k >= 4:
                        t.wait_ge(esm, k - 3)
                    sl = op_slots[k % 4]
                    for ac in range(16):
                        ins = t.matmul(sl, OW[:, 2048 * ac + 128 * dc:2048 * ac + 128 * (dc + 1)],
                                       ctxf[:, 1024 * ac + 512 * th:1024 * ac + 512 * (th + 1)],
                                       start=(ac == 0), stop=(ac == 15))
                    ins.then_inc(sm, 1)
            TE(_op)
            C["opps"].n += 2
            W(VE, C["xshp%d" % (dc % 2)], 16 * (dc // 2 + 1))
            W(VE, C["opps"], C["opps"].n)
            def _opev(v, dc=dc, esm=C["opev"].sem, bsm=C["x1bc"].sem):
                v.tensor_scalar(xch[dc % 2][:, :], xch[dc % 2][:, :], gab[:, dc:dc + 1], None, ALU.add)
                for th in range(2):
                    sl = op_slots[(2 * dc + th) % 4]
                    v.scalar_tensor_tensor(xch[dc % 2][:, 512 * th:512 * (th + 1)], sl,
                                           gap[:, dc:dc + 1],
                                           xch[dc % 2][:, 512 * th:512 * (th + 1)],
                                           ALU.mult, ALU.add).then_inc(esm, 1)
                v.tensor_copy(x1b[:, 1024 * dc:1024 * (dc + 1)], xch[dc % 2][:, :]).then_inc(bsm, 1)
            VE(_opev)
            C["opev"].n += 2
            C["x1bc"].n += 1
            W(YE, C["x1bc"], C["x1bc"].n)
            C["x1s%d" % (dc % 2)].n += 16
            YE(lambda y, dc=dc, sm=C["x1s%d" % (dc % 2)].sem: y.dma_start(
                out=x1_dram[dc, :, :], in_=xch[dc % 2][:, :]).then_inc(sm, 16))
            if dc + 2 < 16:
                emit_xsh(dc + 2)
            W(SE, C["x1bc"], C["x1bc"].n)
            def _sq2(s, dc=dc, ism=C["st2"].sem, wsm=C["st2r"].sem, wv=2 * dc):
                if dc >= 1:
                    s.wait_ge(wsm, wv)
                s.activation(xsq[0], x1b[:, 1024 * dc:1024 * dc + 512], AF.Square,
                             bias=c_zero[:, 0:1], scale=c_one[:, 0:1]).then_inc(ism, 1)
                s.activation(xsq[1], x1b[:, 1024 * dc + 512:1024 * (dc + 1)], AF.Square,
                             bias=c_zero[:, 0:1], scale=c_one[:, 0:1]).then_inc(ism, 1)
            SE(_sq2)
            C["st2"].n += 2
            W(TE, C["x1bc"], C["x1bc"].n)
            def _st2(t, dc=dc, ism=C["st2"].sem, iv0=C["st2"].n - 2, rsm=C["st2r"].sem):
                t.matmul(PC[0:1, 0:512], onec_bf[:, 0:1], x1b[:, 1024 * dc:1024 * dc + 512],
                         start=(dc == 0), stop=(dc == 15))
                t.matmul(PU[0:1, 0:512], onec_bf[:, 0:1], x1b[:, 1024 * dc + 512:1024 * (dc + 1)],
                         start=(dc == 0), stop=(dc == 15))
                t.wait_ge(ism, iv0 + 1)
                t.matmul(SB2[0:1, 512:1024], onec_bf[:, 0:1], xsq[0],
                         start=(dc == 0), stop=(dc == 15)).then_inc(rsm, 1)
                t.wait_ge(ism, iv0 + 2)
                t.matmul(SB2[0:1, 1024:1536], onec_bf[:, 0:1], xsq[1],
                         start=(dc == 0), stop=(dc == 15)).then_inc(rsm, 1)
            TE(_st2)
            C["st2r"].n += 2
        n_st2r = C["st2r"].n

        W(VE, C["st2r"], n_st2r)
        def _ln2(v, sm=C["ln2"].sem):
            v.tensor_scalar(st4[0:1, 0:512], PC[0:1, 0:512], -1.0 / D, None, ALU.mult)
            v.tensor_scalar(st4[0:1, 512:1024], PU[0:1, 0:512], -1.0 / D, None, ALU.mult)
            v.tensor_scalar(r2row[0:1, 0:512], SB2[0:1, 512:1024], 1.0 / D, None, ALU.mult)
            v.tensor_scalar(r2row[0:1, 512:1024], SB2[0:1, 1024:1536], 1.0 / D, None, ALU.mult)
            v.drain()
            v.tensor_copy(nmu2[0:1, :], st4[0:1, :])
            v.tensor_mul(st4[0:1, :], st4[0:1, :], st4[0:1, :])
            v.drain()
            v.tensor_sub(r2row[0:1, :], r2row[0:1, :], st4[0:1, :])
            v.drain()
            v.tensor_scalar(r2row[0:1, :], r2row[0:1, :], EPS, None, ALU.add)
            v.drain()
            v.reciprocal(st4[0:1, :], r2row[0:1, :])
            v.drain().then_inc(sm, 1)
        C["ln2"].n += 1
        VE(_ln2)
        W(SE, C["ln2"], C["ln2"].n)
        C["ln2"].n += 1
        SE(lambda s, sm=C["ln2"].sem: s.activation(
            st4[0:1, :], st4[0:1, :], AF.Sqrt, bias=c_zero[0:1, 0:1], scale=c_one[0:1, 0:1])
            .then_inc(sm, 1))
        W(VE, C["ln2"], C["ln2"].n)
        def _newt2(v, sm=C["ln2"].sem):
            v.tensor_mul(zfin[0:1, 0:512], st4[0:1, 0:512], st4[0:1, 0:512])
            v.tensor_mul(zacc[0:1, 0:512], st4[0:1, 512:1024], st4[0:1, 512:1024])
            v.drain()
            v.tensor_mul(zfin[0:1, 0:512], zfin[0:1, 0:512], r2row[0:1, 0:512])
            v.tensor_mul(zacc[0:1, 0:512], zacc[0:1, 0:512], r2row[0:1, 512:1024])
            v.drain()
            v.tensor_scalar(zfin[0:1, 0:512], zfin[0:1, 0:512], -0.5, 1.5, ALU.mult, ALU.add)
            v.tensor_scalar(zacc[0:1, 0:512], zacc[0:1, 0:512], -0.5, 1.5, ALU.mult, ALU.add)
            v.drain()
            v.tensor_mul(r2row[0:1, 0:512], st4[0:1, 0:512], zfin[0:1, 0:512])
            v.tensor_mul(r2row[0:1, 512:1024], st4[0:1, 512:1024], zacc[0:1, 0:512])
            v.drain().then_inc(sm, 1)
        C["ln2"].n += 1
        VE(_newt2)
        n_ln2f = C["ln2"].n

        W(TE, C["ln2"], n_ln2f)
        def _r2t(t, sm=C["mm"].sem):
            for j in range(8):
                ins = t.transpose(PU[:, 256 + j:257 + j], r2row[0:1, 128 * j:128 * (j + 1)],
                                  idf[0:1, 0:1])
            ins.then_inc(sm, 1)
        C["mm"].n += 1
        TE(_r2t)
        W(VE, C["mm"], C["mm"].n)
        C["ev"].n += 1
        VE(lambda v, sm=C["ev"].sem: (v.tensor_copy(r2_T[:, :], PU[:, 256:264]),
                                      v.drain().then_inc(sm, 1))[-1])
        rbt0 = C["rbt"].n
        rbe0 = C["rbe"].n
        ev0 = C["ev"].n
        W(VE, C["ev"], ev0)
        W(VE, C["x1s0"], C["x1s0"].n)
        W(VE, C["x1s1"], C["x1s1"].n)
        for b in range(2):
            def _r2rep(v, b=b, tsm=C["rbt"].sem, esm=C["rbe"].sem, rbe0=rbe0,
                       evs=C["ev"].sem, ev0=ev0):
                for j in range(4):
                    jj = 4 * b + j
                    if jj >= 2:
                        v.wait_ge(esm, rbe0 + jj - 1)
                    v.tensor_scalar(at2[jj % 2], ones128[:, :], r2_T[:, jj:jj + 1], None, ALU.mult) \
                        .then_inc(tsm, 1)
            VE(_r2rep)
            C["rbt"].n += 4
            W(VE, C["rbe"], rbe0 + 4 * (b + 1))
            C["ev"].n += 1
            VE(lambda v, b=b, sm=C["ev"].sem: (
                v.tensor_copy(r2B[:, 512 * b:512 * (b + 1)], PC[:, :]),
                v.drain().then_inc(sm, 1))[-1])
        def _r2tr(t, tsm=C["rbt"].sem, rbt0=rbt0, esm=C["rbe"].sem,
                  evs=C["ev"].sem, ev0=ev0):
            for b in range(2):
                if b == 1:
                    t.wait_ge(evs, ev0 + 1)
                for j in range(4):
                    jj = 4 * b + j
                    t.wait_ge(tsm, rbt0 + jj + 1)
                    t.transpose(PC[:, 128 * j:128 * (j + 1)], at2[jj % 2], idf[:, :]) \
                        .then_inc(esm, 1)
        TE(_r2tr)
        C["rbe"].n += 8
        n_ev_r2b = C["ev"].n

        W(TE, C["ev"], n_ev_r2b)
        W(TE, C["prep"], n_prep_all)
        def _lgt(t, sm=C["ffa"].sem):
            for th in range(2):
                for dc in range(16):
                    t.matmul(SA[0:8, 512 * th:512 * (th + 1)], rw_sb[:, 8 * dc:8 * (dc + 1)],
                             x1b[:, 1024 * dc + 512 * th:1024 * dc + 512 * (th + 1)],
                             start=(dc == 0), stop=False)
                ins = t.matmul(SA[0:8, 512 * th:512 * (th + 1)], wsrr[0:1, :],
                               nmu2[0:1, 512 * th:512 * (th + 1)], start=False, stop=True)
            ins.then_inc(sm, 1)
        C["ffa"].n += 1
        TE(_lgt)
        def _dn(t, sm=C["ffa"].sem):
            for th in range(2):
                for dc in range(16):
                    t.matmul(SB2[:, 512 * th:512 * (th + 1)], Wd_sb[:, 128 * dc:128 * (dc + 1)],
                             x1b[:, 1024 * dc + 512 * th:1024 * dc + 512 * (th + 1)],
                             start=(dc == 0), stop=False)
                ins = t.matmul(SB2[:, 512 * th:512 * (th + 1)], wsd[0:1, :],
                               nmu2[0:1, 512 * th:512 * (th + 1)], start=False, stop=True)
            ins.then_inc(sm, 1)
        C["ffa"].n += 1
        TE(_dn)
        n_ffa_mm = C["ffa"].n
        b0_lev = C["ffb"].n
        a0_lev = C["ffa"].n
        for th in range(2):
            W(VE, C["ffa"], n_ffa_mm - 1)
            def _lev(v, th=th, sm=C["ffb"].sem, tsm=C["ffa"].sem, need=a0_lev + th):
                if th == 1:
                    v.wait_ge(tsm, need)
                v.tensor_mul(lstg[0:8, 0:512], SA[0:8, 512 * th:512 * (th + 1)],
                             r2B[0:8, 512 * th:512 * (th + 1)])
                v.tensor_scalar(lstg[0:8, 0:512], lstg[0:8, 0:512], bR[0:8, 0:1], None, ALU.add) \
                    .then_inc(sm, 1)
            VE(_lev)
            C["ffb"].n += 1
        def _ltr(t, sm=C["ffa"].sem, bsm=C["ffb"].sem, b0=b0_lev):
            for th in range(2):
                t.wait_ge(bsm, b0 + th + 1)
                for j in range(4):
                    ins = t.transpose(PU[:, 8 * (4 * th + j):8 * (4 * th + j + 1)],
                                      lstg[0:8, 128 * j:128 * (j + 1)], idf[0:8, 0:8])
                ins.then_inc(sm, 1)
        TE(_ltr)
        C["ffa"].n += 2
        W(VE, C["ffa"], C["ffa"].n)
        C["ev"].n += 1
        VE(lambda v, sm=C["ev"].sem: (v.tensor_copy(ltok[:, :], PU[:, 0:64]),
                                      v.drain().then_inc(sm, 1))[-1])
        n_ev_ltok = C["ev"].n
        W(VE, C["ev"], n_ev_ltok)
        def _gate(v, sm=C["ffb"].sem):
            for tt in range(8):
                v.tensor_reduce(lw[:, tt:tt + 1], ltok[:, 8 * tt:8 * (tt + 1)], AX.X, ALU.max)
            v.drain()
            for tt in range(8):
                v.tensor_scalar(ltok[:, 8 * tt:8 * (tt + 1)], ltok[:, 8 * tt:8 * (tt + 1)],
                                lw[:, tt:tt + 1], None, ALU.subtract)
            v.drain().then_inc(sm, 1)
        VE(_gate)
        C["ffb"].n += 1
        W(SE, C["ffb"], C["ffb"].n)
        C["ffc"].n += 1
        SE(lambda s, sm=C["ffc"].sem: s.activation(ltok[:, :], ltok[:, :], AF.Exp,
                                                   bias=c_zero[:, 0:1], scale=c_one[:, 0:1])
           .then_inc(sm, 1))
        W(VE, C["ffc"], C["ffc"].n)
        def _top2(v, sm=C["ffb"].sem):
            for tt in range(8):
                v.tensor_reduce(lw[:, 8 + tt:9 + tt], ltok[:, 8 * tt:8 * (tt + 1)], AX.X, ALU.add)
            v.drain()
            v.reciprocal(lw[:, 8:16], lw[:, 8:16])
            v.drain()
            for tt in range(8):
                v.tensor_scalar(ltok[:, 8 * tt:8 * (tt + 1)], ltok[:, 8 * tt:8 * (tt + 1)],
                                lw[:, 8 + tt:9 + tt], None, ALU.mult)
            v.drain()
            for tt in range(8):
                v.tensor_reduce(lw[:, 16 + tt:17 + tt], ltok[:, 8 * tt:8 * (tt + 1)], AX.X, ALU.max)
            v.drain()
            for tt in range(8):
                v.tensor_scalar(msk[:, 8 * tt:8 * (tt + 1)], ltok[:, 8 * tt:8 * (tt + 1)],
                                lw[:, 16 + tt:17 + tt], None, ALU.is_equal)
            v.drain()
            for tt in range(8):
                v.tensor_scalar(msk[:, 8 * tt:8 * (tt + 1)], msk[:, 8 * tt:8 * (tt + 1)],
                                -1e9, None, ALU.mult)
            v.drain()
            v.tensor_add(msk[:, 0:64], msk[:, 0:64], ltok[:, :])
            v.drain()
            for tt in range(8):
                v.tensor_reduce(lw[:, 24 + tt:25 + tt], msk[:, 8 * tt:8 * (tt + 1)], AX.X, ALU.max)
            v.drain()
            for tt in range(8):
                v.tensor_scalar(msk[:, 8 * tt:8 * (tt + 1)], ltok[:, 8 * tt:8 * (tt + 1)],
                                lw[:, 24 + tt:25 + tt], None, ALU.is_ge)
            v.drain()
            v.tensor_mul(ltok[:, :], ltok[:, :], msk[:, 0:64])
            v.drain().then_inc(sm, 1)
        VE(_top2)
        C["ffb"].n += 1
        n_ffb_gate = C["ffb"].n
        if dev:
            W(YE, C["ffb"], n_ffb_gate)
            C["dbgs"].n += 16
            YE(lambda y, sm=C["dbgs"].sem: y.dma_start(out=dbg["gate"], in_=ltok[:, :]).then_inc(sm, 16))
        W(TE, C["ffb"], n_ffb_gate)
        def _gtr(t, sm=C["ffa"].sem, esm=C["ev"].sem, eb=C["ev"].n):
            for b in range(2):
                if b == 1:
                    t.wait_ge(esm, eb + 1)
                for j in range(4):
                    ins = t.transpose(PU[0:8, 128 * j:128 * (j + 1)],
                                      ltok[:, 8 * (4 * b + j):8 * (4 * b + j + 1)], idf[:, :])
                ins.then_inc(sm, 1)
        TE(_gtr)
        C["ffa"].n += 2
        for b in range(2):
            W(VE, C["ffa"], C["ffa"].n - 1 + b)
            C["ev"].n += 1
            VE(lambda v, b=b, sm=C["ev"].sem: (
                v.tensor_copy(gateT[0:8, 512 * b:512 * (b + 1)], PU[0:8, 0:512]),
                v.drain().then_inc(sm, 1))[-1])
        n_ev_gT = C["ev"].n

        W(VE, C["ffa"], n_ffa_mm)
        def _hev(v, sm=C["ffc"].sem):
            v.tensor_mul(het[0], SB2[:, 0:512], r2B[:, 0:512])
            v.tensor_mul(het[1], SB2[:, 512:1024], r2B[:, 512:1024]).then_inc(sm, 1)
        VE(_hev)
        C["ffc"].n += 1
        W(SE, C["ffc"], C["ffc"].n)
        def _silu(s, sm=C["ffc"].sem):
            s.activation(a_sb[:, 0:512], het[0], AF.Silu, bias=bD[:, 0:1], scale=c_one[:, 0:1])
            s.activation(a_sb[:, 512:1024], het[1], AF.Silu, bias=bD[:, 0:1], scale=c_one[:, 0:1]) \
                .then_inc(sm, 1)
        SE(_silu)
        C["ffc"].n += 1
        n_ffc_silu = C["ffc"].n
        W(TE, C["ev"], n_ev_gT)
        def _gx(t, sm=C["ffa"].sem):
            t.matmul(SA[:, 0:512], emat[0:8, :], gateT[0:8, 0:512], start=True, stop=True)
            t.matmul(SA[:, 512:1024], emat[0:8, :], gateT[0:8, 512:1024], start=True, stop=True) \
                .then_inc(sm, 1)
        TE(_gx)
        C["ffa"].n += 1
        W(VE, C["ffa"], C["ffa"].n)
        W(VE, C["ffc"], n_ffc_silu)
        def _a2(v, sm=C["ffb"].sem):
            v.tensor_mul(a2_sb[:, 0:512], a_sb[:, 0:512], SA[:, 0:512])
            v.tensor_mul(a2_sb[:, 512:1024], a_sb[:, 512:1024], SA[:, 512:1024]).then_inc(sm, 1)
        VE(_a2)
        C["ffb"].n += 1
        n_ffb_a2 = C["ffb"].n

        up_slots = [SA[:, 0:512], SA[:, 512:1024], SA[:, 1024:1536],
                    SB2[:, 0:512], SB2[:, 512:1024], SB2[:, 1024:1536]]
        def emit_x1r(dc):
            W(YE, C["x1s0"], C["x1s0"].n)
            W(YE, C["x1s1"], C["x1s1"].n)
            W(YE, C["rbe"], C["rbe"].n)
            if dc >= 2:
                W(YE, C["outp%d" % (dc % 2)], 16 * (dc // 2))
            C["x1rp%d" % (dc % 2)].n += 16
            YE(lambda y, dc=dc, sm=C["x1rp%d" % (dc % 2)].sem: y.dma_start(
                out=xch[dc % 2][:, :], in_=x1_dram[dc, :, :]).then_inc(sm, 16))
        emit_x1r(0)
        emit_x1r(1)
        W(TE, C["ffb"], n_ffb_a2)
        for dc in range(16):
            def _up(t, dc=dc, sm=C["ffd"].sem, esm=C["upev"].sem, eb=C["upev"].n - 2 * dc):
                for th in range(2):
                    k = 2 * dc + th
                    if k >= 6:
                        t.wait_ge(esm, k - 5)
                    ins = t.matmul(up_slots[k % 6], U_sb[:, 128 * dc:128 * (dc + 1)],
                                   a2_sb[:, 512 * th:512 * (th + 1)], start=True, stop=True)
                ins.then_inc(sm, 1)
            TE(_up)
            C["ffd"].n += 1
            W(VE, C["ffd"], C["ffd"].n)
            W(VE, C["x1rp%d" % (dc % 2)], 16 * (dc // 2 + 1))
            def _res2(v, dc=dc, esm=C["upev"].sem):
                for th in range(2):
                    sl = up_slots[(2 * dc + th) % 6]
                    v.scalar_tensor_tensor(xch[dc % 2][:, 512 * th:512 * (th + 1)], sl,
                                           gbp[:, dc:dc + 1],
                                           xch[dc % 2][:, 512 * th:512 * (th + 1)],
                                           ALU.mult, ALU.add).then_inc(esm, 1)
            VE(_res2)
            C["upev"].n += 2
            W(YE, C["upev"], C["upev"].n)
            C["outp%d" % (dc % 2)].n += 16
            YE(lambda y, dc=dc, sm=C["outp%d" % (dc % 2)].sem: y.dma_start(
                out=out_d[128 * dc:128 * (dc + 1), :], in_=xch[dc % 2][:, :]).then_inc(sm, 16))
            if dc + 2 < 16:
                emit_x1r(dc + 2)

        if dev:
            W(YE, C["x1s0"], C["x1s0"].n)
            C["dbgs"].n += 16
            YE(lambda y, sm=C["dbgs"].sem: y.dma_start(out=dbg["x1"], in_=x1_dram[0, :, :]).then_inc(sm, 16))
            W(GE, C["outp0"], C["outp0"].n)
            W(GE, C["outp1"], C["outp1"].n)
            def _dffn(g, sm=C["dbgs"].sem):
                g.dma_start(out=dbg["x1f"], in_=x1_dram[:, :, :]).then_inc(sm, 16)
                g.dma_start(out=dbg["a2"], in_=a2_sb[:, :]).then_inc(sm, 16)
                g.dma_start(out=dbg["het"], in_=SC[:, 3840:4864]).then_inc(sm, 16)
                g.dma_start(out=dbg["r2b"], in_=r2B[:, :]).then_inc(sm, 16)
                g.dma_start(out=dbg["gbp"], in_=gbp[:, :]).then_inc(sm, 16)
                g.dma_start(out=dbg["usb"], in_=U_sb[:, :]).then_inc(sm, 16)
            GE(_dffn)
            C["dbgs"].n += 96

        W(YE, C["outp0"], C["outp0"].n)
        W(YE, C["outp1"], C["outp1"].n)
        if dev:
            W(YE, C["dbgs"], C["dbgs"].n)

        block = es.enter_context(nc.Block())

        @block.tensor
        def _(t):
            for f in prog["t"]:
                f(t)

        @block.vector
        def _(v):
            for f in prog["v"]:
                f(v)

        @block.scalar
        def _(s):
            for f in prog["s"]:
                f(s)

        @block.gpsimd
        def _(g):
            for f in prog["g"]:
                f(g)

        @block.sync
        def _(y):
            for f in prog["y"]:
                f(y)

    return nc


def _prep_inputs(inputs):
    x = np.asarray(inputs["L0_attn_hidden_states"])
    Wqkv = np.asarray(inputs["L0_attn_Wqkv_w"])
    bqkv = np.asarray(inputs["L0_attn_Wqkv_b"])
    ow = np.asarray(inputs["L0_attn_out_w"])
    outb = np.asarray(inputs["L0_attn_out_b"])
    rw = np.asarray(inputs["L0_ffn_router_w"])
    dw = np.asarray(inputs["L0_ffn_down_w"])
    uw = np.asarray(inputs["L0_ffn_up_w"])
    cv = np.asarray(inputs["L0_c"])
    wm = np.asarray(inputs["L0_wmod"])

    xT = np.ascontiguousarray(x[0].T)
    owT = np.ascontiguousarray(ow.T).reshape(16, 128, D)
    rwT = np.ascontiguousarray(rw.T).reshape(16, 128, 8)
    wdT = np.ascontiguousarray(dw.reshape(128, D).T).reshape(16, 128, 128)
    wuT = np.ascontiguousarray(np.transpose(uw, (0, 2, 1)).reshape(128, D))
    cvec = np.ascontiguousarray(cv.reshape(16, 128).T)
    outbv = np.ascontiguousarray(outb.reshape(16, 128).T)
    idbf = np.eye(128, dtype=ml_dtypes.bfloat16)
    idf = np.eye(128, dtype=np.float32)
    emat = np.zeros((8, 128), dtype=ml_dtypes.bfloat16)
    for e in range(8):
        emat[e, 16 * e:16 * (e + 1)] = 1.0

    in_maps = []
    for i in range(NCORE):
        qs = Wqkv[256 * i:256 * (i + 1), :]
        ks = Wqkv[D + 256 * i:D + 256 * (i + 1), :]
        vs = Wqkv[2 * D + 256 * i:2 * D + 256 * (i + 1), :]
        wqi = np.ascontiguousarray(np.concatenate([qs, ks, vs], axis=0).T)
        bqi = np.concatenate([bqkv[256 * i:256 * (i + 1)],
                              bqkv[D + 256 * i:D + 256 * (i + 1)],
                              bqkv[2 * D + 256 * i:2 * D + 256 * (i + 1)]])
        bqi = np.ascontiguousarray(bqi.reshape(6, 128).T)
        wms = wm[:, 1536 * i:1536 * (i + 1)]
        wmt = np.ascontiguousarray(wms.reshape(16, 128, 12, 128).transpose(0, 2, 1, 3))
        in_maps.append({
            "xfull": xT,
            "xsh": np.ascontiguousarray(xT[:, TOK * i:TOK * (i + 1)]),
            "wq": wqi, "bq": bqi, "wmod": wmt, "cvec": cvec,
            "ow": owT, "outb": outbv, "rw": rwT, "wd": wdT, "wu": wuT,
            "idbf": idbf, "idf": idf, "emat": emat,
        })
    return in_maps


_CACHE = {}


def kernel(**inputs):
    in_maps = _prep_inputs(inputs)
    if "nc" not in _CACHE:
        _CACHE["nc"] = build_nc(dev=DEV)
    res = run_bass_kernel_spmd(_CACHE["nc"], in_maps, core_ids=list(range(NCORE)))
    out = np.empty((S, D), dtype=np.float32)
    for i in range(NCORE):
        out[TOK * i:TOK * (i + 1), :] = res.results[i]["out"].T
    if DEV:
        kernel.last_res = res
    return out.reshape(1, S, D)


# version 08
# speedup vs baseline: 1.4991x; 1.0167x over previous
import contextlib
import numpy as np
import ml_dtypes

import concourse.bass as bass
import concourse.mybir as mybir
from concourse.bass_utils import run_bass_kernel_spmd

F32 = mybir.dt.float32
BF16 = mybir.dt.bfloat16
I32 = mybir.dt.int32
AF = mybir.ActivationFunctionType
ALU = mybir.AluOpType
AX = mybir.AxisListType

NCORE = 8
D = 2048
S = 8192
HPC = 2
TOK = S // NCORE
NTC = 16
QB = 512
NQB = S // QB
GRP = [3] * 21 + [1]
NG = len(GRP)
SHIFT = -16.0
ISQ = 1.0 / float(np.sqrt(128.0))
EPS = 1e-5

DEV = False


class Ctr:
    def __init__(self, sem):
        self.sem = sem
        self.n = 0


def build_nc(dev=False):
    nc = bass.Bass("TRN2", target_bir_lowering=False, enable_partition_id=True)

    xfull = nc.dram_tensor("xfull", [D, S], F32, kind="ExternalInput").ap()
    xsh_d = nc.dram_tensor("xsh", [D, TOK], F32, kind="ExternalInput").ap()
    wq_d = nc.dram_tensor("wq", [D, 768], F32, kind="ExternalInput").ap()
    bq_d = nc.dram_tensor("bq", [128, 6], F32, kind="ExternalInput").ap()
    wmod_d = nc.dram_tensor("wmod", [16, 12, 128, 128], F32, kind="ExternalInput").ap()
    cvec_d = nc.dram_tensor("cvec", [128, 16], F32, kind="ExternalInput").ap()
    ow_d = nc.dram_tensor("ow", [16, 128, D], F32, kind="ExternalInput").ap()
    outb_d = nc.dram_tensor("outb", [128, 16], F32, kind="ExternalInput").ap()
    rw_d = nc.dram_tensor("rw", [16, 128, 8], F32, kind="ExternalInput").ap()
    wd_d = nc.dram_tensor("wd", [16, 128, 128], F32, kind="ExternalInput").ap()
    wu_d = nc.dram_tensor("wu", [128, D], F32, kind="ExternalInput").ap()
    idbf_d = nc.dram_tensor("idbf", [128, 128], BF16, kind="ExternalInput").ap()
    idf_d = nc.dram_tensor("idf", [128, 128], F32, kind="ExternalInput").ap()
    emat_d = nc.dram_tensor("emat", [8, 128], BF16, kind="ExternalInput").ap()

    out_d = nc.dram_tensor("out", [D, TOK], F32, kind="ExternalOutput").ap()

    mod_in = nc.dram_tensor("mod_in", [1536], F32)
    mod_out = nc.dram_tensor("mod_out", [12288], F32)
    st_in = nc.dram_tensor("st_in", [2048], F32)
    st_out = nc.dram_tensor("st_out", [16384], F32)
    q_dram = nc.dram_tensor("q_dram", [HPC, 128, S], BF16)
    qn_dram = nc.dram_tensor("qn_dram", [HPC, S], F32)
    ctx_in = nc.dram_tensor("ctx_in", [HPC * 128, S], BF16)
    ctx_outA = nc.dram_tensor("ctx_outA", [NCORE * 128, S], BF16)
    ctx_outB = nc.dram_tensor("ctx_outB", [NCORE * 128, S], BF16)
    x1_dram = nc.dram_tensor("x1_dram", [16, 128, TOK], F32)

    dbg = {}
    if dev:
        dbg["mod"] = nc.dram_tensor("dbg_mod", [128, 96], F32, kind="ExternalOutput").ap()
        dbg["modin"] = nc.dram_tensor("dbg_modin", [12, 128], F32, kind="ExternalOutput").ap()
        dbg["q"] = nc.dram_tensor("dbg_q", [2, 128, 512], F32, kind="ExternalOutput").ap()
        dbg["zacc"] = nc.dram_tensor("dbg_zacc", [128, 512], F32, kind="ExternalOutput").ap()
        dbg["cf"] = nc.dram_tensor("dbg_cf", [128, 16 * TOK], F32, kind="ExternalOutput").ap()
        dbg["zinvb"] = nc.dram_tensor("dbg_zinvb", [128, 512], F32, kind="ExternalOutput").ap()
        dbg["pp"] = nc.dram_tensor("dbg_pp", [128, 1536], F32, kind="ExternalOutput").ap()
        dbg["r64"] = nc.dram_tensor("dbg_r64", [64, 128], F32, kind="ExternalOutput").ap()
        dbg["k"] = nc.dram_tensor("dbg_k", [128, 512], F32, kind="ExternalOutput").ap()
        dbg["kf"] = nc.dram_tensor("dbg_kf", [128, 2 * S], F32, kind="ExternalOutput").ap()
        dbg["qf"] = nc.dram_tensor("dbg_qf", [2, 128, S], F32, kind="ExternalOutput").ap()
        dbg["v"] = nc.dram_tensor("dbg_v", [128, 512], F32, kind="ExternalOutput").ap()
        dbg["ctx"] = nc.dram_tensor("dbg_ctx", [256, S], F32, kind="ExternalOutput").ap()
        dbg["x1"] = nc.dram_tensor("dbg_x1", [128, TOK], F32, kind="ExternalOutput").ap()
        dbg["gate"] = nc.dram_tensor("dbg_gate", [128, 64], F32, kind="ExternalOutput").ap()
        dbg["x1f"] = nc.dram_tensor("dbg_x1f", [16, 128, TOK], F32, kind="ExternalOutput").ap()
        dbg["a2"] = nc.dram_tensor("dbg_a2", [128, TOK], F32, kind="ExternalOutput").ap()
        dbg["het"] = nc.dram_tensor("dbg_het", [128, TOK], F32, kind="ExternalOutput").ap()
        dbg["r2b"] = nc.dram_tensor("dbg_r2b", [128, TOK], F32, kind="ExternalOutput").ap()
        dbg["gbp"] = nc.dram_tensor("dbg_gbp", [128, 16], F32, kind="ExternalOutput").ap()
        dbg["usb"] = nc.dram_tensor("dbg_usb", [128, D], F32, kind="ExternalOutput").ap()

    es = contextlib.ExitStack()
    with es:
        sb = lambda nm, sh, dt: es.enter_context(nc.sbuf_tensor(nm, sh, dt))
        psf = lambda nm, sh: es.enter_context(nc.psum_tensor(nm, sh, F32))

        RK = sb("RK", [128, 16384], BF16)
        RV = sb("RV", [128, 16384], BF16)
        RW = sb("RW", [128, 36864], BF16)
        RP = sb("RP", [128, 5632], BF16)
        SC = sb("SC", [128, 4864], F32)
        silc = sb("silc", [128, 16], F32)
        modv = sb("modv", [128, 96], F32)
        modbf = sb("modbf", [128, 96], BF16)
        ca1 = sb("ca1", [128, 16], F32)
        ca1s = sb("ca1s", [128, 16], F32)
        cb1 = sb("cb1", [128, 16], F32)
        ca1bf = sb("ca1bf", [128, 16], BF16)
        ca1sbf = sb("ca1sbf", [128, 16], BF16)
        cb1bf = sb("cb1bf", [128, 16], BF16)
        gap = sb("gap", [128, 16], F32)
        gbp = sb("gbp", [128, 16], F32)
        gab = sb("gab", [128, 16], F32)
        tmpa = sb("tmpa", [128, 16], F32)
        tmpb = sb("tmpb", [128, 16], F32)
        bqkv = sb("bqkv", [128, 6], F32)
        wsrF = sb("wsrF", [128, 768], BF16)
        r_T = sb("r_T", [128, 64], F32)
        r64 = sb("r64", [64, 128], F32)
        m96 = sb("m96", [96, 128], F32)
        idbf = sb("idbf_s", [128, 128], BF16)
        idf = sb("idf_s", [128, 128], F32)
        emat = sb("emat_s", [8, 128], BF16)
        onec_bf = sb("onec_bf", [128, 1], BF16)
        onerow_bf = sb("onerow_bf", [1, 128], BF16)
        e0 = sb("e0", [128, 128], BF16)
        mkneg = sb("mkneg", [1, 2], F32)
        ones128 = sb("ones128", [128, 128], F32)
        c_one = sb("c_one", [128, 1], F32)
        c_zero = sb("c_zero", [128, 1], F32)
        c_shift = sb("c_shift", [128, 1], F32)
        strow = sb("strow", [1, 2048], F32)
        qnst = strow[0:1, 0:1024]
        qnrow = strow[0:1, 1024:2048]
        st4 = sb("st4", [1, 1024], F32)
        qld = sb("qld", [128, 2 * QB], BF16)
        qst = sb("qst", [128, 2 * 512], BF16)
        cstg = sb("cstg", [128, 2 * QB], BF16)
        Wd_sb = sb("Wd_sb", [128, 2048], BF16)
        U_sb = sb("U_sb", [128, 2048], BF16)
        rw_sb = sb("rw_sb", [128, 128], BF16)
        bD = sb("bD", [128, 1], F32)
        bR = sb("bR", [8, 1], F32)
        wsd = sb("wsd", [1, 128], BF16)
        wsrr = sb("wsrr", [1, 8], BF16)
        nmu2 = sb("nmu2", [1, TOK], BF16)
        r2row = sb("r2row", [1, TOK], F32)
        r2_T = sb("r2_T", [128, 8], F32)

        SA = psf("SA", [128, 1536])
        SB2 = psf("SB2", [128, 1536])
        PC = psf("PC", [128, 512])
        PU = psf("PU", [128, 512])

        xb = [RW[:, 0:8192], RW[:, 8192:16384]]
        Wsb = RW[:, 16384:28672]
        negmu = RW[0:1, 28672:36864]
        OW = RW[:, 0:32768]
        Ksb = RK
        x1b = RK
        vraw = RV
        ctxf = RV
        Pp = [RP[:, 0:1536], RP[:, 1536:3072]]
        ztree = [RP[:, 3072 + 512 * k:3584 + 512 * k] for k in range(5)]
        a_sb = RP[:, 0:1024]
        a2_sb = RP[:, 1024:2048]
        xsq = [RP[:, 2048 + 512 * k:2560 + 512 * k] for k in range(2)]
        gateT = RP[0:8, 3072:4096]
        wm_st = [SC[:, 0:1536], SC[:, 1536:3072]]
        rB = [SC[:, 0:512], SC[:, 512:1024]]
        qtmp = [SC[:, 1024:1536], SC[:, 1536:2048]]
        at2 = [SC[:, 2048:2176], SC[:, 2176:2304]]
        zat = [SC[:, 2304 + 128 * k:2432 + 128 * k] for k in range(4)]
        zacc = SC[:, 0:512]
        zfin = SC[:, 512:1024]
        zinvb = SC[:, 1024:1536]
        r2B = SC[:, 0:1024]
        xch = [SC[:, 1024:2048], SC[:, 2048:3072]]
        ltok = SC[:, 3072:3136]
        lw = SC[:, 3136:3168]
        msk = SC[:, 3168:3232]
        lstg = SC[0:8, 3232:3744]
        het = [SC[:, 3840 + 512 * k:4352 + 512 * k] for k in range(2)]
        modrow = SC[0:1, 3328:4864]

        names = ["wc", "stx0", "stx1", "modp0", "modp1", "xp0", "xp1",
                 "qs0", "qs1", "qldp0", "qldp1", "cstp0", "cstp1",
                 "xshp0", "xshp1", "x1s0", "x1s1", "x1rp0", "x1rp1",
                 "outp0", "outp1",
                 "w", "x", "mod", "cc", "st", "prep", "mm", "ev", "evq", "evk",
                 "evv", "qdma", "rbt", "rbe", "sq", "sqr",
                 "qk", "exp", "avg", "tree", "ctxev", "vz1", "vz2", "vz3",
                 "tz1", "tz2", "cst", "cdma", "qldd",
                 "owd", "ctxl", "opps", "xshd", "opev", "x1d", "x1bc", "st2",
                 "st2r", "ln2", "ffa", "ffb", "ffc", "ffd", "upev", "x1r",
                 "outd", "dbgs", "misc", "nmv", "ncon"]
        C = {nm: Ctr(es.enter_context(nc.semaphore("s_" + nm))) for nm in names}

        prog = {k: [] for k in "tvsgy"}
        TE, VE, SE, GE, YE = (lambda f: prog["t"].append(f), lambda f: prog["v"].append(f),
                              lambda f: prog["s"].append(f), lambda f: prog["g"].append(f),
                              lambda f: prog["y"].append(f))

        def W(q, ctr, val=None):
            v = ctr.n if val is None else val
            q(lambda e, s=ctr.sem, v=v: e.wait_ge(s, v))

        def _init(v):
            v.memset(onec_bf[:, :], 1.0)
            v.memset(onerow_bf[:, :], 1.0)
            v.memset(e0[:, :], 0.0)
            v.memset(e0[0:1, :], 1.0)
            v.memset(wsrF[:, :], 0.0)
            v.memset(RW[:, 28672:36864], 0.0)
            v.memset(ones128[:, :], 1.0)
            v.memset(c_one[:, :], 1.0)
            v.memset(c_zero[:, :], 0.0)
            v.memset(c_shift[:, :], SHIFT)
        VE(_init)
        C["misc"].n += 1
        VE(lambda v, s=C["misc"].sem, k=C["misc"].n: v.drain().then_inc(s, 1))
        n_init = C["misc"].n

        def gd(dst, src, ctr):
            ctr.n += 16
            GE(lambda g, dst=dst, src=src, s=ctr.sem: g.dma_start(out=dst, in_=src).then_inc(s, 16))

        gd(idbf[:, :], idbf_d, C["wc"])
        gd(idf[:, :], idf_d, C["wc"])
        gd(emat[:, :], emat_d, C["wc"])
        gd(silc[:, :], cvec_d, C["wc"])
        n_w0 = C["wc"].n

        W(SE, C["wc"], n_w0)
        W(SE, C["misc"], n_init)
        C["prep"].n += 1
        SE(lambda s, sm=C["prep"].sem: s.activation(
            silc[:, :], silc[:, :], AF.Silu, bias=c_zero[:, 0:1], scale=c_one[:, 0:1]).then_inc(sm, 1))
        n_prep_sil = C["prep"].n

        for dc in range(16):
            if dc >= 2:
                W(GE if dc % 2 == 0 else YE, C["mm"], dc - 1)
            ctr = C["modp%d" % (dc % 2)]
            ctr.n += 16
            q_emit = GE if dc % 2 == 0 else YE
            q_emit(lambda q, dc=dc, sm=ctr.sem: q.dma_start(
                out=wm_st[dc % 2].rearrange("p (m f) -> p m f", f=128),
                in_=wmod_d[dc, :, :, :].transpose([1, 0, 2])).then_inc(sm, 16))
        for sc_i in range(2):
            gd(xb[sc_i].rearrange("p (c f) -> p c f", f=512),
               xsh_d.rearrange("(c p) t -> p c t", p=128)[:, :, 512 * sc_i:512 * (sc_i + 1)],
               C["stx%d" % sc_i])
        gd(Wsb.rearrange("p (c f) -> p c f", f=768),
           wq_d.rearrange("(c p) f -> p c f", p=128), C["w"])
        gd(bqkv[:, :], bq_d, C["w"])
        gd(gab[:, :], outb_d, C["w"])
        gd(Wd_sb.rearrange("p (c f) -> p c f", f=128), wd_d.transpose([1, 0, 2]), C["w"])
        gd(rw_sb.rearrange("p (c f) -> p c f", f=8), rw_d.transpose([1, 0, 2]), C["w"])
        gd(U_sb[:, :], wu_d, C["w"])
        n_w1 = C["w"].n
        W(TE, C["prep"], n_prep_sil)
        for dc in range(16):
            W(TE, C["modp%d" % (dc % 2)], 16 * (dc // 2 + 1))
            def _mv(t, dc=dc, sm=C["mm"].sem):
                for part in range(3):
                    ins = t.matmul(SA[0:1, 512 * part:512 * (part + 1)],
                                   silc[:, dc:dc + 1],
                                   wm_st[dc % 2][:, 512 * part:512 * (part + 1)],
                                   start=(dc == 0), stop=(dc == 15))
                ins.then_inc(sm, 1)
            TE(_mv)
            C["mm"].n += 1
        n_mm_mod = C["mm"].n
        W(VE, C["mm"], n_mm_mod)
        C["ev"].n += 1
        VE(lambda v, sm=C["ev"].sem: v.tensor_copy(modrow[0:1, :], SA[0:1, 0:1536]).then_inc(sm, 1))
        W(YE, C["ev"], C["ev"].n)
        C["st"].n += 16
        YE(lambda y, sm=C["st"].sem: y.dma_start(
            out=mod_in[:].rearrange("(a k) -> a k", a=1), in_=modrow[0:1, :]).then_inc(sm, 16))
        W(GE, C["st"], C["st"].n)
        C["cc"].n += 1
        GE(lambda g, sm=C["cc"].sem: g.collective_compute(
            "AllGather", ALU.bypass, replica_groups=[list(range(NCORE))],
            ins=[mod_in.ap().opt()], outs=[mod_out.ap().opt()]).then_inc(sm, 1))
        W(GE, C["cc"], C["cc"].n)
        C["st"].n += 16
        GE(lambda g, sm=C["st"].sem: g.dma_start(
            out=m96[:, :], in_=mod_out[:].rearrange("(c p) -> c p", p=128)).then_inc(sm, 16))
        W(TE, C["st"], C["st"].n)
        C["mm"].n += 1
        TE(lambda t, sm=C["mm"].sem: t.transpose(SA[:, 128:224], m96[:, :], idf[0:96, 0:96]).then_inc(sm, 1))
        W(VE, C["mm"], C["mm"].n)
        C["ev"].n += 1
        VE(lambda v, sm=C["ev"].sem: v.tensor_copy(modv[:, :], SA[:, 128:224]).then_inc(sm, 1))

        W(VE, C["w"], n_w1)
        def _modprep(v, sm=C["prep"].sem):
            v.tensor_scalar(ca1[:, :], modv[:, 16:32], 1.0, None, ALU.add)
            v.tensor_scalar(cb1[:, :], modv[:, 64:80], 1.0, None, ALU.add)
            v.drain()
            v.tensor_scalar(ca1s[:, :], ca1[:, :], ISQ, None, ALU.mult)
            for (sl, dst) in ((slice(32, 48), gap), (slice(80, 96), gbp)):
                v.tensor_scalar(tmpa[:, :], modv[:, sl], 0.25, None, ALU.max)
                v.tensor_scalar(tmpb[:, :], modv[:, sl], -0.25, None, ALU.min)
                v.tensor_scalar(dst[:, :], modv[:, sl], 0.0, None, ALU.is_ge)
                v.drain()
                v.tensor_sub(tmpa[:, :], tmpa[:, :], tmpb[:, :])
                v.drain()
                v.tensor_mul(tmpa[:, :], dst[:, :], tmpa[:, :])
                v.drain()
                v.tensor_add(dst[:, :], tmpa[:, :], tmpb[:, :])
                v.drain()
            v.tensor_mul(gab[:, :], gap[:, :], gab[:, :])
            v.tensor_copy(modbf[:, :], modv[:, :])
            v.tensor_copy(ca1bf[:, :], ca1[:, :])
            v.tensor_copy(ca1sbf[:, :], ca1s[:, :])
            v.tensor_copy(cb1bf[:, :], cb1[:, :])
            v.drain().then_inc(sm, 1)
        C["prep"].n += 1
        VE(_modprep)
        n_prep_mod = C["prep"].n

        if dev:
            W(YE, C["prep"], n_prep_mod)
            C["dbgs"].n += 16
            YE(lambda y, sm=C["dbgs"].sem: y.dma_start(out=dbg["mod"], in_=modv[:, :]).then_inc(sm, 16))
            C["dbgs"].n += 16
            YE(lambda y, sm=C["dbgs"].sem: y.dma_start(
                out=dbg["modin"], in_=mod_in[:].rearrange("(m p) -> m p", p=128)).then_inc(sm, 16))

        W(TE, C["w"], n_w1)
        W(TE, C["prep"], n_prep_mod)
        def _bmv(t, sm=C["mm"].sem):
            for ft in range(6):
                for dc in range(16):
                    ins = t.matmul(PU[:, 16 + ft:17 + ft],
                                   Wsb[:, 768 * dc + 128 * ft:768 * dc + 128 * (ft + 1)],
                                   modbf[:, dc:dc + 1], start=(dc == 0), stop=(dc == 15))
            for ft in range(6):
                carhs = ca1sbf if ft < 2 else ca1bf
                for dc in range(16):
                    ins = t.matmul(PC[:, ft:ft + 1],
                                   Wsb[:, 768 * dc + 128 * ft:768 * dc + 128 * (ft + 1)],
                                   carhs[:, dc:dc + 1], start=(dc == 0), stop=(dc == 15))
            ins.then_inc(sm, 1)
        C["mm"].n += 1
        TE(_bmv)
        n_mm_bmv = C["mm"].n
        W(VE, C["mm"], n_mm_bmv)
        def _bprep(v, sm=C["ev"].sem):
            v.tensor_add(bqkv[:, :], bqkv[:, :], PU[:, 16:22])
            v.drain()
            v.tensor_scalar(bqkv[:, 0:2], bqkv[:, 0:2], ISQ, None, ALU.mult)
            v.drain().then_inc(sm, 1)
        C["ev"].n += 1
        VE(_bprep)
        W(VE, C["mm"], n_mm_bmv)
        C["ev"].n += 1
        VE(lambda v, sm=C["ev"].sem: v.tensor_copy(tmpb[:, 0:6], PC[:, 0:6]).then_inc(sm, 1))
        W(TE, C["ev"], C["ev"].n)
        for ft in range(6):
            C["mm"].n += 1
            TE(lambda t, ft=ft, sm=C["mm"].sem: t.transpose(
                SB2[0:1, 128 * ft:128 * (ft + 1)], tmpb[:, ft:ft + 1], idf[:, :]).then_inc(sm, 1))
        W(VE, C["mm"], C["mm"].n)
        def _wsrf(v, sm=C["ev"].sem):
            v.tensor_copy(wsrF[0:1, :], SB2[0:1, 0:768])
            v.drain().then_inc(sm, 1)
        C["ev"].n += 1
        VE(_wsrf)
        n_ev_wsr = C["ev"].n

        def _fmv(t, sm=C["mm"].sem):
            for dc in range(16):
                ins = t.matmul(PU[:, 32:33], Wd_sb[:, 128 * dc:128 * (dc + 1)],
                               modbf[:, 48 + dc:49 + dc], start=(dc == 0), stop=(dc == 15))
            for dc in range(16):
                ins = t.matmul(PU[:, 34:35], Wd_sb[:, 128 * dc:128 * (dc + 1)],
                               cb1bf[:, dc:dc + 1], start=(dc == 0), stop=(dc == 15))
            for dc in range(16):
                ins = t.matmul(PU[0:8, 36:37], rw_sb[:, 8 * dc:8 * (dc + 1)],
                               modbf[:, 48 + dc:49 + dc], start=(dc == 0), stop=(dc == 15))
            for dc in range(16):
                ins = t.matmul(PU[0:8, 38:39], rw_sb[:, 8 * dc:8 * (dc + 1)],
                               cb1bf[:, dc:dc + 1], start=(dc == 0), stop=(dc == 15))
            ins.then_inc(sm, 1)
        C["mm"].n += 1
        TE(_fmv)
        n_mm_fmv = C["mm"].n
        W(VE, C["mm"], n_mm_fmv)
        C["ev"].n += 1
        VE(lambda v, sm=C["ev"].sem: (v.tensor_copy(bD[:, :], PU[:, 32:33]),
                                      v.tensor_copy(bR[:, :], PU[0:8, 36:37]),
                                      v.drain().then_inc(sm, 1))[-1])
        C["ev"].n += 1
        VE(lambda v, sm=C["ev"].sem: (v.tensor_copy(tmpb[:, 6:7], PU[:, 34:35]),
                                      v.tensor_copy(tmpb[0:8, 7:8], PU[0:8, 38:39]),
                                      v.drain().then_inc(sm, 1))[-1])
        W(TE, C["ev"], C["ev"].n)
        C["mm"].n += 1
        TE(lambda t, sm=C["mm"].sem: t.transpose(SB2[0:1, 768:896], tmpb[:, 6:7], idf[:, :]).then_inc(sm, 1))
        C["mm"].n += 1
        TE(lambda t, sm=C["mm"].sem: t.transpose(SB2[0:1, 896:904], tmpb[0:8, 7:8], idf[0:8, 0:8]).then_inc(sm, 1))
        W(VE, C["mm"], C["mm"].n)
        def _wsd(v, sm=C["prep"].sem):
            v.tensor_copy(wsd[0:1, :], SB2[0:1, 768:896])
            v.tensor_copy(wsrr[0:1, :], SB2[0:1, 896:904])
            for dc in range(16):
                v.tensor_scalar(Wd_sb[:, 128 * dc:128 * (dc + 1)], Wd_sb[:, 128 * dc:128 * (dc + 1)],
                                cb1[:, dc:dc + 1], None, ALU.mult)
                v.tensor_scalar(rw_sb[:, 8 * dc:8 * (dc + 1)], rw_sb[:, 8 * dc:8 * (dc + 1)],
                                cb1[:, dc:dc + 1], None, ALU.mult)
            v.drain().then_inc(sm, 1)
        C["prep"].n += 1
        VE(_wsd)
        def _wscale(v, sm=C["prep"].sem):
            for dc in range(16):
                v.tensor_scalar(Wsb[:, 768 * dc:768 * dc + 256], Wsb[:, 768 * dc:768 * dc + 256],
                                ca1s[:, dc:dc + 1], None, ALU.mult)
                v.tensor_scalar(Wsb[:, 768 * dc + 256:768 * dc + 768],
                                Wsb[:, 768 * dc + 256:768 * dc + 768],
                                ca1[:, dc:dc + 1], None, ALU.mult)
            v.drain().then_inc(sm, 1)
        C["prep"].n += 1
        VE(_wscale)
        n_prep_all = C["prep"].n

        for sc_i in range(2):
            W(SE, C["stx%d" % sc_i], 16)
            def _sq(s, sc_i=sc_i, wsm=C["sqr"].sem, ism=C["sq"].sem, base=C["sq"].n):
                for dc in range(16):
                    if 16 * sc_i + dc >= 2:
                        s.wait_ge(wsm, 16 * sc_i + dc - 1)
                    s.activation(xsq[dc % 2], xb[sc_i][:, 512 * dc:512 * (dc + 1)], AF.Square,
                                 bias=c_zero[:, 0:1], scale=c_one[:, 0:1]).then_inc(ism, 1)
            SE(_sq)
            C["sq"].n += 16
            W(TE, C["stx%d" % sc_i], 16)
            if sc_i == 0:
                W(TE, C["mm"], C["mm"].n)
            else:
                W(TE, C["ln2"], 1)
            def _stmv(t, sc_i=sc_i, qsm=C["sq"].sem, rsm=C["sqr"].sem):
                for dc in range(16):
                    t.wait_ge(qsm, 16 * sc_i + dc + 1)
                    t.matmul(SA[0:1, 0:512], onec_bf[:, 0:1], xb[sc_i][:, 512 * dc:512 * (dc + 1)],
                             start=(dc == 0), stop=(dc == 15))
                    t.matmul(SA[0:1, 512:1024], onec_bf[:, 0:1], xsq[dc % 2],
                             start=(dc == 0), stop=(dc == 15)).then_inc(rsm, 1)
            TE(_stmv)
            C["sqr"].n += 16
            W(VE, C["sqr"], C["sqr"].n)
            def _stf(v, sc_i=sc_i, sm=C["ln2"].sem):
                slm = slice(512 * sc_i, 512 * (sc_i + 1))
                v.tensor_scalar(strow[0:1, slm], SA[0:1, 0:512], -1.0 / D, None, ALU.mult)
                v.tensor_scalar(st4[0:1, 0:512], SA[0:1, 512:1024], 1.0 / D, None, ALU.mult)
                v.drain()
                v.tensor_mul(st4[0:1, 512:1024], strow[0:1, slm], strow[0:1, slm])
                v.drain()
                v.tensor_sub(st4[0:1, 0:512], st4[0:1, 0:512], st4[0:1, 512:1024])
                v.drain()
                v.tensor_scalar(st4[0:1, 0:512], st4[0:1, 0:512], EPS, None, ALU.add)
                v.drain()
                v.reciprocal(st4[0:1, 512:1024], st4[0:1, 0:512])
                v.drain().then_inc(sm, 1)
            C["ln2"].n += 1
            VE(_stf)
            W(SE, C["ln2"], C["ln2"].n)
            C["ln2"].n += 1
            SE(lambda s, sm=C["ln2"].sem: s.activation(
                st4[0:1, 512:1024], st4[0:1, 512:1024], AF.Sqrt,
                bias=c_zero[0:1, 0:1], scale=c_one[0:1, 0:1]).then_inc(sm, 1))
            W(VE, C["ln2"], C["ln2"].n)
            def _newt(v, sc_i=sc_i, sm=C["ln2"].sem):
                slr = slice(1024 + 512 * sc_i, 1024 + 512 * (sc_i + 1))
                v.tensor_mul(strow[0:1, slr], st4[0:1, 512:1024], st4[0:1, 512:1024])
                v.drain()
                v.tensor_mul(strow[0:1, slr], strow[0:1, slr], st4[0:1, 0:512])
                v.drain()
                v.tensor_scalar(strow[0:1, slr], strow[0:1, slr], -0.5, 1.5, ALU.mult, ALU.add)
                v.drain()
                v.tensor_mul(strow[0:1, slr], strow[0:1, slr], st4[0:1, 512:1024])
                v.drain().then_inc(sm, 1)
            C["ln2"].n += 1
            VE(_newt)
        W(YE, C["ln2"], C["ln2"].n)
        C["st"].n += 16
        YE(lambda y, sm=C["st"].sem: y.dma_start(
            out=st_in[:].rearrange("(a k) -> a k", a=1), in_=strow[0:1, :]).then_inc(sm, 16))
        n_stats_st = C["st"].n
        W(GE, C["st"], C["st"].n)
        C["cc"].n += 1
        GE(lambda g, sm=C["cc"].sem: g.collective_compute(
            "AllGather", ALU.bypass, replica_groups=[list(range(NCORE))],
            ins=[st_in.ap().opt()], outs=[st_out.ap().opt()]).then_inc(sm, 1))
        W(GE, C["cc"], C["cc"].n)
        W(GE, C["misc"], n_init)
        C["st"].n += 16
        GE(lambda g, sm=C["st"].sem: g.dma_start(
            out=negmu.rearrange("p (r k) -> p r k", r=8),
            in_=st_out[:].rearrange("(r t k) -> t r k", r=8, t=2)[0:1, :, :]).then_inc(sm, 16))
        for rr in range(8):
            C["st"].n += 16
            GE(lambda g, rr=rr, sm=C["st"].sem: g.dma_start(
                out=r64[8 * rr:8 * (rr + 1), :],
                in_=st_out[:].rearrange("(r t j k) -> r t j k", r=8, t=2, j=8)[rr, 1, :, :]).then_inc(sm, 16))
        n_st_rows = C["st"].n
        W(TE, C["st"], n_st_rows)
        C["mm"].n += 1
        TE(lambda t, sm=C["mm"].sem: t.transpose(PU[:, 64:128], r64[:, :], idf[0:64, 0:64]).then_inc(sm, 1))
        W(VE, C["mm"], C["mm"].n)
        C["ev"].n += 1
        VE(lambda v, sm=C["ev"].sem: v.tensor_copy(r_T[:, :], PU[:, 64:128]).then_inc(sm, 1))
        n_ev_rT = C["ev"].n

        if dev:
            W(YE, C["st"], n_st_rows)
            C["dbgs"].n += 16
            YE(lambda y, sm=C["dbgs"].sem: y.dma_start(out=dbg["r64"], in_=r64[:, :]).then_inc(sm, 16))

        qk_slots = [SA[:, 0:512], SA[:, 512:1024], SA[:, 1024:1536], SB2[:, 0:512]]
        v_slots = [SB2[:, 512:1024], SB2[:, 1024:1536]]
        qsqb = [RP[:, 0:512], RP[:, 512:1024]]
        qn_st_base = C["st"].n
        for tc in range(NTC):
            if tc < 2:
                W(GE, C["sqr"], 32)
            else:
                W(GE, C["evv"], 4 * (tc - 1))
            gd(xb[tc % 2].rearrange("p (c f) -> p c f", f=512),
               xfull.rearrange("(c p) t -> p c t", p=128)[:, :, 512 * tc:512 * (tc + 1)],
               C["xp%d" % (tc % 2)])
        for tc in range(NTC):
            W(VE, C["ev"], n_ev_rT)
            def _repl(v, tc=tc, tsm=C["rbt"].sem, esm=C["rbe"].sem,
                      tb=C["rbt"].n, eb=C["rbe"].n):
                for j in range(4):
                    if 4 * tc + j >= 2:
                        v.wait_ge(esm, eb + j - 2 + 1)
                    v.tensor_scalar(at2[j % 2], ones128[:, :], r_T[:, 4 * tc + j:4 * tc + j + 1],
                                    None, ALU.mult).then_inc(tsm, 1)
            VE(_repl)
            C["rbt"].n += 4
            def _trb(t, tc=tc, tsm=C["rbt"].sem, esm=C["rbe"].sem,
                     tb=C["rbt"].n - 4, vsm=C["ev"].sem, need_ev=C["ev"].n):
                for j in range(4):
                    t.wait_ge(tsm, tb + j + 1)
                    if tc >= 1 and j == 0:
                        t.wait_ge(vsm, need_ev)
                    t.transpose(PC[:, 128 * j:128 * (j + 1)], at2[j % 2], idf[:, :]).then_inc(esm, 1)
            if tc >= 1:
                pass
            TE(_trb)
            C["rbe"].n += 4
            W(VE, C["rbe"], C["rbe"].n)
            C["ev"].n += 1
            VE(lambda v, tc=tc, sm=C["ev"].sem: v.tensor_copy(rB[tc % 2][:, :], PC[:, :]).then_inc(sm, 1))

            W(TE, C["xp%d" % (tc % 2)], 16 * (tc // 2 + 1))
            if tc == 0:
                W(TE, C["prep"], n_prep_all)
                W(TE, C["ev"], n_ev_wsr)
                W(TE, C["ln2"], 6)
            def _qk(t, tc=tc, sm=C["qk"].sem, esm=C["evk"].sem):
                for ft in range(4):
                    gidx = 4 * tc + ft
                    if gidx >= 4:
                        t.wait_ge(esm, gidx - 3)
                    slot = qk_slots[gidx % 4]
                    for dc in range(16):
                        t.matmul(slot, Wsb[:, 768 * dc + 128 * ft:768 * dc + 128 * (ft + 1)],
                                 xb[tc % 2][:, 512 * dc:512 * (dc + 1)], start=(dc == 0), stop=False)
                    t.matmul(slot, wsrF[:, 128 * ft:128 * (ft + 1)],
                             RW[:, 28672 + 512 * tc:28672 + 512 * (tc + 1)],
                             start=False, stop=True).then_inc(sm, 1)
            TE(_qk)
            C["qk"].n += 4
            def _vmm(t, tc=tc, sm=C["evv"].sem, psm=C["mm"].sem, pb=C["mm"].n):
                for tt in range(4):
                    gidx = 4 * tc + tt
                    if gidx >= 2:
                        t.wait_ge(sm, gidx - 1)
                    slot = v_slots[tt % 2]
                    for dc in range(16):
                        t.matmul(slot[:, 0:256],
                                 xb[tc % 2][:, 512 * dc + 128 * tt:512 * dc + 128 * (tt + 1)],
                                 Wsb[:, 768 * dc + 512:768 * dc + 768], start=(dc == 0), stop=False)
                    t.matmul(slot[:, 0:256],
                             RW[:, 28672 + 512 * tc + 128 * tt:28672 + 512 * tc + 128 * (tt + 1)],
                             wsrF[:, 512:768], start=False, stop=True).then_inc(psm, 1)
            TE(_vmm)
            C["mm"].n += 4
            def _evqk(v, tc=tc, qsm=C["qk"].sem, qb_=C["qk"].n - 4,
                      ksm=C["evk"].sem, qesm=C["evq"].sem,
                      dsm=(C["qs0"].sem, C["qs1"].sem), sqs=C["sq"].sem,
                      rbready=C["ev"].n):
                for ft in range(4):
                    v.wait_ge(qsm, qb_ + ft + 1)
                    slot = qk_slots[(4 * tc + ft) % 4]
                    tb = qtmp[ft % 2]
                    ins = v.tensor_mul(tb, slot, rB[tc % 2][:, :])
                    if ft < 2:
                        ins.then_inc(ksm, 1)
                        if tc >= 1:
                            v.wait_ge(dsm[ft], 16 * tc)
                            v.wait_ge(sqs, 4 * tc + 30)
                        v.tensor_scalar(qst[:, 512 * ft:512 * (ft + 1)], tb,
                                        bqkv[:, ft:ft + 1], None, ALU.add).then_inc(qesm, 1)
                    else:
                        h = ft - 2
                        v.tensor_scalar(Ksb[:, 8192 * h + 512 * tc:8192 * h + 512 * (tc + 1)],
                                        tb, bqkv[:, ft:ft + 1], None, ALU.add).then_inc(ksm, 1)
            VE(_evqk)
            C["evq"].n += 2
            C["evk"].n += 4
            def _evv(v, tc=tc, psm=C["mm"].sem, pb=C["mm"].n - 4, sm=C["evv"].sem):
                for tt in range(4):
                    v.wait_ge(psm, pb + tt + 1)
                    slot = v_slots[tt % 2]
                    kt = 4 * tc + tt
                    v.tensor_scalar(vraw[:, 256 * kt:256 * (kt + 1)], slot[:, 0:256],
                                    r_T[:, kt:kt + 1], None, ALU.mult).then_inc(sm, 1)
            VE(_evv)
            C["evv"].n += 4
            for ft in range(2):
                W(YE, C["evq"], C["evq"].n - 1 + ft)
                C["qs%d" % ft].n += 16
                YE(lambda y, tc=tc, ft=ft, sm=C["qs%d" % ft].sem: y.dma_start(
                    out=q_dram[ft, :, 512 * tc:512 * (tc + 1)],
                    in_=qst[:, 512 * ft:512 * (ft + 1)]).then_inc(sm, 16))
            for j in range(4):
                idx = 4 * tc + j
                def _nsq(s, tc=tc, j=j, idx=idx, sm=C["sq"].sem,
                         eq=C["evq"].sem, ek=C["evk"].sem, nm=C["nmv"].sem):
                    if idx >= 2:
                        s.wait_ge(nm, idx - 1)
                    if j < 2:
                        s.wait_ge(eq, 2 * tc + j + 1)
                    else:
                        s.wait_ge(ek, 4 * tc + j + 1)
                    src_ = (qst[:, 512 * j:512 * (j + 1)] if j < 2 else
                            Ksb[:, 8192 * (j - 2) + 512 * tc:8192 * (j - 2) + 512 * (tc + 1)])
                    s.activation(qsqb[idx % 2], src_, AF.Square,
                                 bias=c_zero[:, 0:1], scale=c_one[:, 0:1]).then_inc(sm, 1)
                SE(_nsq)
                C["sq"].n += 1
                def _nmv(t, idx=idx, qv=C["sq"].n, qs=C["sq"].sem, sm=C["nmv"].sem,
                         co=C["ncon"].sem, evs=C["ev"].sem):
                    t.wait_ge(qs, qv)
                    if idx == 0:
                        t.wait_ge(evs, n_ev_rT)
                    else:
                        t.wait_ge(co, idx)
                    t.matmul(PU[0:1, 0:512], onec_bf[:, 0:1], qsqb[idx % 2],
                             start=True, stop=True).then_inc(sm, 1)
                TE(_nmv)
                C["nmv"].n += 1
                if j < 2:
                    def _nsqrt(s, mv=C["nmv"].n, ms=C["nmv"].sem,
                               sm=C["ncon"].sem, sts=C["st"].sem, stv=C["st"].n):
                        s.wait_ge(ms, mv)
                        s.wait_ge(sts, stv)
                        s.activation(qnst[0:1, 0:512], PU[0:1, 0:512], AF.Sqrt,
                                     bias=c_zero[0:1, 0:1], scale=c_one[0:1, 0:1]).then_inc(sm, 1)
                    SE(_nsqrt)
                    C["ncon"].n += 1
                    W(YE, C["ncon"], C["ncon"].n)
                    C["st"].n += 16
                    YE(lambda y, tc=tc, j=j, sm=C["st"].sem: y.dma_start(
                        out=qn_dram[j, 512 * tc:512 * (tc + 1)].rearrange("(a k) -> a k", a=1),
                        in_=qnst[0:1, 0:512]).then_inc(sm, 16))
                else:
                    def _nkm(v, tc=tc, j=j, mv=C["nmv"].n, ms=C["nmv"].sem,
                             sm=C["ncon"].sem, sts=C["st"].sem, stv=n_stats_st):
                        v.wait_ge(ms, mv)
                        v.wait_ge(sts, stv)
                        v.tensor_reduce(
                            qnst[0:1, 512 + 16 * (j - 2) + tc:513 + 16 * (j - 2) + tc],
                            PU[0:1, 0:512], AX.X, ALU.max).then_inc(sm, 1)
                    VE(_nkm)
                    C["ncon"].n += 1
        n_sq_all = C["sq"].n
        n_evk_all = C["evk"].n
        n_evv_all = C["evv"].n
        n_qk_all = C["qk"].n

        if dev:
            W(GE, C["evk"], n_evk_all)
            W(GE, C["evv"], n_evv_all)
            W(GE, C["qs0"], C["qs0"].n)
            W(GE, C["qs1"], C["qs1"].n)
            C["dbgs"].n += 48
            def _dq(g, sm=C["dbgs"].sem):
                g.dma_start(out=dbg["k"], in_=Ksb[:, 0:512]).then_inc(sm, 16)
                g.dma_start(out=dbg["v"], in_=vraw[:, 0:512]).then_inc(sm, 16)
                g.dma_start(out=dbg["q"], in_=q_dram[:, :, 0:512]).then_inc(sm, 16)
                g.dma_start(out=dbg["kf"], in_=Ksb[:, :]).then_inc(sm, 16)
                g.dma_start(out=dbg["qf"], in_=q_dram[:, :, :]).then_inc(sm, 16)
            GE(_dq)
            C["dbgs"].n += 32

        W(VE, C["ncon"], 64)
        C["ln2"].n += 1
        VE(lambda v, sm=C["ln2"].sem: (
            v.tensor_reduce(qnst[0:1, 544:545], qnst[0:1, 512:528], AX.X, ALU.max),
            v.tensor_reduce(qnst[0:1, 545:546], qnst[0:1, 528:544], AX.X, ALU.max),
            v.drain().then_inc(sm, 1))[-1])
        W(SE, C["ln2"], C["ln2"].n)
        C["ln2"].n += 1
        SE(lambda s, sm=C["ln2"].sem: s.activation(
            qnst[0:1, 544:546], qnst[0:1, 544:546], AF.Sqrt,
            bias=c_zero[0:1, 0:1], scale=c_one[0:1, 0:1]).then_inc(sm, 1))
        W(VE, C["ln2"], C["ln2"].n)
        C["ln2"].n += 1
        VE(lambda v, sm=C["ln2"].sem: (
            v.tensor_scalar(mkneg[0:1, :], qnst[0:1, 544:546], -1.0, None, ALU.mult),
            v.drain().then_inc(sm, 1))[-1])
        n_mk = C["ln2"].n

        W(GE, C["evv"], n_evv_all)
        W(GE, C["evk"], n_evk_all)
        gd(OW.rearrange("p (c f) -> p c f", f=2048), ow_d.transpose([1, 0, 2]), C["owd"])
        n_owd = C["owd"].n

        kt_grp = []
        acc = 0
        for gsz in GRP:
            kt_grp.append((acc, gsz))
            acc += gsz
        rings = [SA, SB2]
        NQI = HPC * NQB

        nb_base = C["evq"].n
        stx_base = {0: C["stx0"].n, 1: C["stx1"].n}
        def emit_qld(qi):
            h, qb = divmod(qi, NQB)
            W(YE, C["qs%d" % h], 16 * (qb + 1))
            W(YE, C["st"], qn_st_base + 16 * (2 * qb + h + 1))
            if qi >= 2:
                W(YE, C["qk"], n_qk_all + (qi - 1) * NG)
                W(YE, C["evq"], nb_base + qi - 1)
            C["qldp%d" % (qi % 2)].n += 16
            YE(lambda y, h=h, qb=qb, qi=qi, sm=C["qldp%d" % (qi % 2)].sem: y.dma_start(
                out=qld[:, QB * (qi % 2):QB * (qi % 2 + 1)],
                in_=q_dram[h, :, QB * qb:QB * (qb + 1)]).then_inc(sm, 16))
            C["stx%d" % (qi % 2)].n += 16
            YE(lambda y, h=h, qb=qb, qi=qi, sm=C["stx%d" % (qi % 2)].sem: y.dma_start(
                out=qnrow[0:1, 512 * (qi % 2):512 * (qi % 2 + 1)],
                in_=qn_dram[h, 512 * qb:512 * (qb + 1)].rearrange("(a k) -> a k", a=1))
                .then_inc(sm, 16))
        def emit_negB(qi):
            h = qi // NQB
            W(VE, C["stx%d" % (qi % 2)], stx_base[qi % 2] + 16 * (qi // 2 + 1))
            W(VE, C["ln2"], n_mk)
            if qi < 2:
                W(VE, C["qs0"], C["qs0"].n)
                W(VE, C["qs1"], C["qs1"].n)
                W(VE, C["sq"], n_sq_all)
            if qi >= 2:
                W(VE, C["qk"], n_qk_all + (qi - 1) * NG)
            def _nb(v, qi=qi, h=h, sm=C["evq"].sem):
                v.tensor_scalar(qst[0:1, 512 * (qi % 2):512 * (qi % 2 + 1)],
                                qnrow[0:1, 512 * (qi % 2):512 * (qi % 2 + 1)],
                                mkneg[0:1, h:h + 1], 70.0, ALU.mult, ALU.add).then_inc(sm, 1)
            VE(_nb)
            C["evq"].n += 1
        emit_qld(0)
        emit_qld(1)

        qk0 = n_qk_all
        for qi in range(NQI):
            for gi, (ktb, gsz) in enumerate(kt_grp):
                G = qi * NG + gi
                W(SE, C["qk"], qk0 + G + 1)
                if G >= 2:
                    W(SE, C["avg"], G - 1)
                    W(SE, C["tree"], G - 1)
                if dev and G == 2:
                    W(SE, C["dbgs"], C["dbgs"].n + 16)
                C["exp"].n += 1
                SE(lambda s, G=G, gsz=gsz, sm=C["exp"].sem: s.activation(
                    Pp[G % 2][:, 0:512 * gsz], rings[G % 2][:, 0:512 * gsz], AF.Exp,
                    bias=c_zero[:, 0:1], scale=c_one[:, 0:1]).then_inc(sm, 1))

        if dev:
            W(GE, C["exp"], 1)
            C["dbgs"].n += 16
            GE(lambda g, sm=C["dbgs"].sem: g.dma_start(out=dbg["pp"], in_=Pp[0][:, 0:1536]).then_inc(sm, 16))
        def emit_ztr(qi):
            W(TE, C["vz1"], qi + 1)
            if dev and qi == 0:
                W(TE, C["dbgs"], C["dbgs"].n)
            if qi >= 1:
                W(TE, C["vz3"], qi)
            def _ztr(t, sm=C["tz1"].sem):
                for j in range(4):
                    ins = t.transpose(PU[:, 128 * j:128 * (j + 1)], zacc[:, 128 * j:128 * (j + 1)],
                                      idf[:, :])
                ins.then_inc(sm, 1)
            TE(_ztr)
            C["tz1"].n += 1
            W(TE, C["vz2"], qi + 1)
            def _ztr2(t, sm=C["tz2"].sem):
                for j in range(4):
                    ins = t.transpose(PU[:, 128 * j:128 * (j + 1)], zat[j], idf[:, :])
                ins.then_inc(sm, 1)
            TE(_ztr2)
            C["tz2"].n += 1
        for qi in range(NQI):
            h, qb = divmod(qi, NQB)
            qslot = qld[:, QB * (qi % 2):QB * (qi % 2 + 1)]
            for gi, (ktb, gsz) in enumerate(kt_grp):
                G = qi * NG + gi
                if G >= 2:
                    W(TE, C["exp"], G - 1)
                if G == 0:
                    W(TE, C["evk"], n_evk_all)
                    W(TE, C["evv"], n_evv_all)
                    W(TE, C["ev"], C["ev"].n)
                if gi == 0:
                    W(TE, C["qldp%d" % (qi % 2)], 16 * (qi // 2 + 1))
                    W(TE, C["evq"], nb_base + qi + 1)
                def _qkg(t, h=h, ktb=ktb, gsz=gsz, G=G, qi=qi, qslot=qslot, sm=C["qk"].sem):
                    nbs = qst[:, 512 * (qi % 2):512 * (qi % 2 + 1)]
                    for j in range(gsz):
                        kt = ktb + j
                        t.matmul(rings[G % 2][:, 512 * j:512 * (j + 1)],
                                 Ksb[:, 8192 * h + 128 * kt:8192 * h + 128 * (kt + 1)],
                                 qslot, start=True, stop=False)
                        ins = t.matmul(rings[G % 2][:, 512 * j:512 * (j + 1)],
                                       e0[:, :], nbs, start=False, stop=True)
                    ins.then_inc(sm, 1)
                TE(_qkg)
                C["qk"].n += 1
                if gi == 1 and qi >= 1:
                    emit_ztr(qi - 1)
                if gi >= 1:
                    Gp = G - 1
                    ktbp, gszp = kt_grp[gi - 1]
                    W(TE, C["exp"], Gp + 1)
                    if gi == 1:
                        W(TE, C["ctxev"], qi)
                    def _avg(t, h=h, ktbp=ktbp, gszp=gszp, Gp=Gp, first=(gi == 1), sm=C["avg"].sem):
                        for j in range(gszp):
                            kt = ktbp + j
                            ins = t.matmul(PC[:, :],
                                           vraw[:, 256 * kt + 128 * h:256 * kt + 128 * (h + 1)],
                                           Pp[Gp % 2][:, 512 * j:512 * (j + 1)],
                                           start=(first and j == 0), stop=False)
                        ins.then_inc(sm, 1)
                    TE(_avg)
                    C["avg"].n += 1
            Gl = qi * NG + NG - 1
            ktbp, gszp = kt_grp[NG - 1]
            W(TE, C["exp"], Gl + 1)
            def _avl(t, h=h, ktbp=ktbp, gszp=gszp, Gl=Gl, sm=C["avg"].sem):
                for j in range(gszp):
                    kt = ktbp + j
                    ins = t.matmul(PC[:, :],
                                   vraw[:, 256 * kt + 128 * h:256 * kt + 128 * (h + 1)],
                                   Pp[Gl % 2][:, 512 * j:512 * (j + 1)],
                                   start=False, stop=(j == gszp - 1))
                ins.then_inc(sm, 1)
            TE(_avl)
            C["avg"].n += 1
            if qi == NQI - 1:
                emit_ztr(qi)

        tree_sched = []
        for qi in range(NQI):
            pend = []
            free = list(range(5))
            persched = []
            for gi, (ktb, gsz) in enumerate(kt_grp):
                ops = []
                s0 = free.pop(0)
                ops.append(("grp", s0, gsz))
                pend.append([1, s0])
                while len(pend) >= 2 and pend[-1][0] == pend[-2][0]:
                    l1, sl1 = pend.pop()
                    l0, sl0 = pend.pop()
                    ops.append(("mrg", sl0, sl1))
                    free.append(sl1)
                    pend.append([l0 + 1, sl0])
                persched.append(ops)
            tree_sched.append((persched, [p[1] for p in pend]))

        emit_negB(0)
        emit_negB(1)
        h0_csts = []
        for qi in range(NQI):
            h, qb = divmod(qi, NQB)
            persched, rem = tree_sched[qi]
            for gi, (ktb, gsz) in enumerate(kt_grp):
                G = qi * NG + gi
                W(VE, C["exp"], G + 1)
                def _tree(v, G=G, gsz=gsz, ops=persched[gi], sm=C["tree"].sem):
                    ins = None
                    for op in ops:
                        if op[0] == "grp":
                            _, s0, gz = op
                            if gz >= 2:
                                ins = v.tensor_add(ztree[s0], Pp[G % 2][:, 0:512], Pp[G % 2][:, 512:1024])
                                if gz == 3:
                                    ins = v.tensor_add(ztree[s0], ztree[s0], Pp[G % 2][:, 1024:1536])
                            else:
                                ins = v.tensor_copy(ztree[s0], Pp[G % 2][:, 0:512])
                        else:
                            _, sl0, sl1 = op
                            ins = v.tensor_add(ztree[sl0], ztree[sl0], ztree[sl1])
                    ins.then_inc(sm, 1)
                VE(_tree)
                C["tree"].n += 1
            def _zfin(v, rem=rem, sm=C["vz1"].sem):
                ins = None
                for sl in rem:
                    if ins is None:
                        ins = v.tensor_copy(zacc[:, :], ztree[sl])
                    else:
                        ins = v.tensor_add(zacc[:, :], zacc[:, :], ztree[sl])
                ins.then_inc(sm, 1)
            VE(_zfin)
            C["vz1"].n += 1
            if dev and qi == 0:
                W(GE, C["vz1"], 1)
                C["dbgs"].n += 16
                GE(lambda g, sm=C["dbgs"].sem: g.dma_start(out=dbg["zacc"], in_=zacc[:, :]).then_inc(sm, 16))
            W(VE, C["tz1"], qi + 1)
            def _zred(v, sm=C["vz2"].sem):
                for j in range(4):
                    v.tensor_reduce(zfin[:, j:j + 1], PU[:, 128 * j:128 * (j + 1)], AX.X, ALU.add)
                v.drain()
                v.reciprocal(zfin[:, 0:4], zfin[:, 0:4])
                v.drain()
                for j in range(4):
                    ins = v.tensor_scalar(zat[j], ones128[:, :], zfin[:, j:j + 1], None, ALU.mult)
                ins.then_inc(sm, 1)
            VE(_zred)
            C["vz2"].n += 1
            W(VE, C["tz2"], qi + 1)
            def _cev(v, h=h, qi=qi, sm3=C["vz3"].sem, cse=C["ctxev"].sem, css=C["cst"].sem,
                     avs=C["avg"].sem, avn=(qi + 1) * NG,
                     dsm=C["cstp%d" % (qi % 2)].sem,
                     dneed=16 * (qi // 2) if qi >= 2 else 0):
                v.tensor_copy(zinvb[:, :], PU[:, 0:512]).then_inc(sm3, 1)
                v.wait_ge(avs, avn)
                v.tensor_mul(zacc[:, :], PC[:, :], zinvb[:, :]).then_inc(cse, 1)
                if dneed:
                    v.wait_ge(dsm, dneed)
                v.tensor_scalar(cstg[:, QB * (qi % 2):QB * (qi % 2 + 1)], zacc[:, :],
                                bqkv[:, 4 + h:5 + h], None, ALU.add).then_inc(css, 1)
            VE(_cev)
            C["vz3"].n += 1
            if dev and qi == 0:
                W(GE, C["vz3"], 1)
                C["dbgs"].n += 16
                GE(lambda g, sm=C["dbgs"].sem: g.dma_start(out=dbg["zinvb"], in_=zinvb[:, :]).then_inc(sm, 16))
            if dev and qi == 1:
                W(VE, C["dbgs"], C["dbgs"].n)
            C["ctxev"].n += 1
            C["cst"].n += 1
            W(YE, C["cst"], C["cst"].n)
            C["cstp%d" % (qi % 2)].n += 16
            YE(lambda y, h=h, qb=qb, qi=qi, sm=C["cstp%d" % (qi % 2)].sem: y.dma_start(
                out=ctx_in[128 * h:128 * (h + 1), QB * qb:QB * (qb + 1)],
                in_=cstg[:, QB * (qi % 2):QB * (qi % 2 + 1)]).then_inc(sm, 16))
            if qi == NQB - 1:
                h0_csts.append((C["cstp0"].n, C["cstp1"].n))
            if qi + 2 < NQI:
                emit_qld(qi + 2)
                emit_negB(qi + 2)

        if dev:
            W(GE, C["cstp0"], C["cstp0"].n)
            W(GE, C["cstp1"], C["cstp1"].n)
            C["dbgs"].n += 16
            GE(lambda g, sm=C["dbgs"].sem: g.dma_start(
                out=dbg["ctx"].rearrange("(a p) t -> a p t", p=128),
                in_=ctx_in[:, :].rearrange("(a p) t -> a p t", p=128)).then_inc(sm, 16))

        W(GE, C["cstp0"], h0_csts[0][0])
        W(GE, C["cstp1"], h0_csts[0][1])
        C["cc"].n += 1
        GE(lambda g, sm=C["cc"].sem: g.collective_compute(
            "AllGather", ALU.bypass, replica_groups=[list(range(NCORE))],
            ins=[ctx_in.ap()[0:128, :].opt()], outs=[ctx_outA.ap().opt()]).then_inc(sm, 1))
        W(GE, C["cstp0"], C["cstp0"].n)
        W(GE, C["cstp1"], C["cstp1"].n)
        C["cc"].n += 1
        GE(lambda g, sm=C["cc"].sem: g.collective_compute(
            "AllGather", ALU.bypass, replica_groups=[list(range(NCORE))],
            ins=[ctx_in.ap()[128:256, :].opt()], outs=[ctx_outB.ap().opt()]).then_inc(sm, 1))
        W(GE, C["cc"], C["cc"].n - 1)
        W(GE, C["avg"], NQI * NG)
        C["ctxl"].n += 16
        def _ctxlA(g, sm=C["ctxl"].sem):
            off = g.partition_id() * TOK
            g.dma_start(out=ctxf.rearrange("p (i h f) -> p i h f", i=8, h=2)[:, :, 0, :],
                        in_=ctx_outA.rearrange("(c p) t -> p c t", p=128)[:, :, bass.ds(off, TOK)]) \
                .then_inc(sm, 16)
        GE(_ctxlA)
        W(GE, C["cc"], C["cc"].n)
        C["ctxl"].n += 16
        def _ctxlB(g, sm=C["ctxl"].sem):
            off = g.partition_id() * TOK
            g.dma_start(out=ctxf.rearrange("p (i h f) -> p i h f", i=8, h=2)[:, :, 1, :],
                        in_=ctx_outB.rearrange("(c p) t -> p c t", p=128)[:, :, bass.ds(off, TOK)]) \
                .then_inc(sm, 16)
        GE(_ctxlB)

        if dev:
            W(GE, C["ctxl"], C["ctxl"].n)
            C["dbgs"].n += 16
            GE(lambda g, sm=C["dbgs"].sem: g.dma_start(out=dbg["cf"], in_=ctxf[:, :]).then_inc(sm, 16))
        op_slots = [SA[:, 0:512], SA[:, 512:1024], SA[:, 1024:1536], SB2[:, 0:512]]
        def emit_xsh(dc):
            if dc >= 2:
                W(YE, C["opev"], 2 * (dc - 1))
                W(YE, C["x1s%d" % (dc % 2)], 16 * (dc // 2))
            W(YE, C["ctxev"], NQI)
            C["xshp%d" % (dc % 2)].n += 16
            YE(lambda y, dc=dc, sm=C["xshp%d" % (dc % 2)].sem: y.dma_start(
                out=xch[dc % 2][:, :],
                in_=xsh_d.rearrange("(c p) t -> p c t", p=128)[:, dc, :]).then_inc(sm, 16))
        emit_xsh(0)
        emit_xsh(1)
        ctxl_half = C["ctxl"].n - 16
        for dc in range(16):
            W(TE, C["ctxl"], ctxl_half if dc == 0 else C["ctxl"].n)
            W(TE, C["owd"], n_owd)
            if dc == 0:
                W(TE, C["vz3"], NQI)
            def _op(t, dc=dc, sm=C["opps"].sem, esm=C["opev"].sem,
                    cls=C["ctxl"].sem, cln=C["ctxl"].n):
                acs = list(range(0, 16, 2)) + list(range(1, 16, 2))
                for th in range(2):
                    k = 2 * dc + th
                    if k >= 4:
                        t.wait_ge(esm, k - 3)
                    sl = op_slots[k % 4]
                    for i, ac in enumerate(acs):
                        if dc == 0 and i == 8:
                            t.wait_ge(cls, cln)
                        ins = t.matmul(sl, OW[:, 2048 * ac + 128 * dc:2048 * ac + 128 * (dc + 1)],
                                       ctxf[:, 1024 * ac + 512 * th:1024 * ac + 512 * (th + 1)],
                                       start=(i == 0), stop=(i == 15))
                    ins.then_inc(sm, 1)
            TE(_op)
            C["opps"].n += 2
            W(VE, C["xshp%d" % (dc % 2)], 16 * (dc // 2 + 1))
            W(VE, C["opps"], C["opps"].n)
            def _opev(v, dc=dc, esm=C["opev"].sem, bsm=C["x1bc"].sem):
                v.tensor_scalar(xch[dc % 2][:, :], xch[dc % 2][:, :], gab[:, dc:dc + 1], None, ALU.add)
                for th in range(2):
                    sl = op_slots[(2 * dc + th) % 4]
                    v.scalar_tensor_tensor(xch[dc % 2][:, 512 * th:512 * (th + 1)], sl,
                                           gap[:, dc:dc + 1],
                                           xch[dc % 2][:, 512 * th:512 * (th + 1)],
                                           ALU.mult, ALU.add).then_inc(esm, 1)
                v.tensor_copy(x1b[:, 1024 * dc:1024 * (dc + 1)], xch[dc % 2][:, :]).then_inc(bsm, 1)
            VE(_opev)
            C["opev"].n += 2
            C["x1bc"].n += 1
            W(YE, C["x1bc"], C["x1bc"].n)
            C["x1s%d" % (dc % 2)].n += 16
            YE(lambda y, dc=dc, sm=C["x1s%d" % (dc % 2)].sem: y.dma_start(
                out=x1_dram[dc, :, :], in_=xch[dc % 2][:, :]).then_inc(sm, 16))
            if dc + 2 < 16:
                emit_xsh(dc + 2)
            W(SE, C["x1bc"], C["x1bc"].n)
            def _sq2(s, dc=dc, ism=C["st2"].sem, wsm=C["st2r"].sem, wv=2 * dc):
                if dc >= 1:
                    s.wait_ge(wsm, wv)
                s.activation(xsq[0], x1b[:, 1024 * dc:1024 * dc + 512], AF.Square,
                             bias=c_zero[:, 0:1], scale=c_one[:, 0:1]).then_inc(ism, 1)
                s.activation(xsq[1], x1b[:, 1024 * dc + 512:1024 * (dc + 1)], AF.Square,
                             bias=c_zero[:, 0:1], scale=c_one[:, 0:1]).then_inc(ism, 1)
            SE(_sq2)
            C["st2"].n += 2
            W(TE, C["x1bc"], C["x1bc"].n)
            def _st2(t, dc=dc, ism=C["st2"].sem, iv0=C["st2"].n - 2, rsm=C["st2r"].sem):
                t.matmul(PC[0:1, 0:512], onec_bf[:, 0:1], x1b[:, 1024 * dc:1024 * dc + 512],
                         start=(dc == 0), stop=(dc == 15))
                t.matmul(PU[0:1, 0:512], onec_bf[:, 0:1], x1b[:, 1024 * dc + 512:1024 * (dc + 1)],
                         start=(dc == 0), stop=(dc == 15))
                t.wait_ge(ism, iv0 + 1)
                t.matmul(SB2[0:1, 512:1024], onec_bf[:, 0:1], xsq[0],
                         start=(dc == 0), stop=(dc == 15)).then_inc(rsm, 1)
                t.wait_ge(ism, iv0 + 2)
                t.matmul(SB2[0:1, 1024:1536], onec_bf[:, 0:1], xsq[1],
                         start=(dc == 0), stop=(dc == 15)).then_inc(rsm, 1)
            TE(_st2)
            C["st2r"].n += 2
        n_st2r = C["st2r"].n

        W(VE, C["st2r"], n_st2r)
        def _ln2(v, sm=C["ln2"].sem):
            v.tensor_scalar(st4[0:1, 0:512], PC[0:1, 0:512], -1.0 / D, None, ALU.mult)
            v.tensor_scalar(st4[0:1, 512:1024], PU[0:1, 0:512], -1.0 / D, None, ALU.mult)
            v.tensor_scalar(r2row[0:1, 0:512], SB2[0:1, 512:1024], 1.0 / D, None, ALU.mult)
            v.tensor_scalar(r2row[0:1, 512:1024], SB2[0:1, 1024:1536], 1.0 / D, None, ALU.mult)
            v.drain()
            v.tensor_copy(nmu2[0:1, :], st4[0:1, :])
            v.tensor_mul(st4[0:1, :], st4[0:1, :], st4[0:1, :])
            v.drain()
            v.tensor_sub(r2row[0:1, :], r2row[0:1, :], st4[0:1, :])
            v.drain()
            v.tensor_scalar(r2row[0:1, :], r2row[0:1, :], EPS, None, ALU.add)
            v.drain()
            v.reciprocal(st4[0:1, :], r2row[0:1, :])
            v.drain().then_inc(sm, 1)
        C["ln2"].n += 1
        VE(_ln2)
        W(SE, C["ln2"], C["ln2"].n)
        C["ln2"].n += 1
        SE(lambda s, sm=C["ln2"].sem: s.activation(
            st4[0:1, :], st4[0:1, :], AF.Sqrt, bias=c_zero[0:1, 0:1], scale=c_one[0:1, 0:1])
            .then_inc(sm, 1))
        W(VE, C["ln2"], C["ln2"].n)
        def _newt2(v, sm=C["ln2"].sem):
            v.tensor_mul(zfin[0:1, 0:512], st4[0:1, 0:512], st4[0:1, 0:512])
            v.tensor_mul(zacc[0:1, 0:512], st4[0:1, 512:1024], st4[0:1, 512:1024])
            v.drain()
            v.tensor_mul(zfin[0:1, 0:512], zfin[0:1, 0:512], r2row[0:1, 0:512])
            v.tensor_mul(zacc[0:1, 0:512], zacc[0:1, 0:512], r2row[0:1, 512:1024])
            v.drain()
            v.tensor_scalar(zfin[0:1, 0:512], zfin[0:1, 0:512], -0.5, 1.5, ALU.mult, ALU.add)
            v.tensor_scalar(zacc[0:1, 0:512], zacc[0:1, 0:512], -0.5, 1.5, ALU.mult, ALU.add)
            v.drain()
            v.tensor_mul(r2row[0:1, 0:512], st4[0:1, 0:512], zfin[0:1, 0:512])
            v.tensor_mul(r2row[0:1, 512:1024], st4[0:1, 512:1024], zacc[0:1, 0:512])
            v.drain().then_inc(sm, 1)
        C["ln2"].n += 1
        VE(_newt2)
        n_ln2f = C["ln2"].n

        W(TE, C["ln2"], n_ln2f)
        def _r2t(t, sm=C["mm"].sem):
            for j in range(8):
                ins = t.transpose(PU[:, 256 + j:257 + j], r2row[0:1, 128 * j:128 * (j + 1)],
                                  idf[0:1, 0:1])
            ins.then_inc(sm, 1)
        C["mm"].n += 1
        TE(_r2t)
        W(VE, C["mm"], C["mm"].n)
        C["ev"].n += 1
        VE(lambda v, sm=C["ev"].sem: (v.tensor_copy(r2_T[:, :], PU[:, 256:264]),
                                      v.drain().then_inc(sm, 1))[-1])
        rbt0 = C["rbt"].n
        rbe0 = C["rbe"].n
        ev0 = C["ev"].n
        W(VE, C["ev"], ev0)
        W(VE, C["x1s0"], C["x1s0"].n)
        W(VE, C["x1s1"], C["x1s1"].n)
        for b in range(2):
            def _r2rep(v, b=b, tsm=C["rbt"].sem, esm=C["rbe"].sem, rbe0=rbe0,
                       evs=C["ev"].sem, ev0=ev0):
                for j in range(4):
                    jj = 4 * b + j
                    if jj >= 2:
                        v.wait_ge(esm, rbe0 + jj - 1)
                    v.tensor_scalar(at2[jj % 2], ones128[:, :], r2_T[:, jj:jj + 1], None, ALU.mult) \
                        .then_inc(tsm, 1)
            VE(_r2rep)
            C["rbt"].n += 4
            W(VE, C["rbe"], rbe0 + 4 * (b + 1))
            C["ev"].n += 1
            VE(lambda v, b=b, sm=C["ev"].sem: (
                v.tensor_copy(r2B[:, 512 * b:512 * (b + 1)], PC[:, :]),
                v.drain().then_inc(sm, 1))[-1])
        def _r2tr(t, tsm=C["rbt"].sem, rbt0=rbt0, esm=C["rbe"].sem,
                  evs=C["ev"].sem, ev0=ev0):
            for b in range(2):
                if b == 1:
                    t.wait_ge(evs, ev0 + 1)
                for j in range(4):
                    jj = 4 * b + j
                    t.wait_ge(tsm, rbt0 + jj + 1)
                    t.transpose(PC[:, 128 * j:128 * (j + 1)], at2[jj % 2], idf[:, :]) \
                        .then_inc(esm, 1)
        TE(_r2tr)
        C["rbe"].n += 8
        n_ev_r2b = C["ev"].n

        W(TE, C["ev"], n_ev_r2b)
        W(TE, C["prep"], n_prep_all)
        def _lgt(t, sm=C["ffa"].sem):
            for th in range(2):
                for dc in range(16):
                    t.matmul(SA[0:8, 512 * th:512 * (th + 1)], rw_sb[:, 8 * dc:8 * (dc + 1)],
                             x1b[:, 1024 * dc + 512 * th:1024 * dc + 512 * (th + 1)],
                             start=(dc == 0), stop=False)
                ins = t.matmul(SA[0:8, 512 * th:512 * (th + 1)], wsrr[0:1, :],
                               nmu2[0:1, 512 * th:512 * (th + 1)], start=False, stop=True)
            ins.then_inc(sm, 1)
        C["ffa"].n += 1
        TE(_lgt)
        def _dn(t, sm=C["ffa"].sem):
            for th in range(2):
                for dc in range(16):
                    t.matmul(SB2[:, 512 * th:512 * (th + 1)], Wd_sb[:, 128 * dc:128 * (dc + 1)],
                             x1b[:, 1024 * dc + 512 * th:1024 * dc + 512 * (th + 1)],
                             start=(dc == 0), stop=False)
                ins = t.matmul(SB2[:, 512 * th:512 * (th + 1)], wsd[0:1, :],
                               nmu2[0:1, 512 * th:512 * (th + 1)], start=False, stop=True)
            ins.then_inc(sm, 1)
        C["ffa"].n += 1
        TE(_dn)
        n_ffa_mm = C["ffa"].n
        b0_lev = C["ffb"].n
        a0_lev = C["ffa"].n
        for th in range(2):
            W(VE, C["ffa"], n_ffa_mm - 1)
            def _lev(v, th=th, sm=C["ffb"].sem, tsm=C["ffa"].sem, need=a0_lev + th):
                if th == 1:
                    v.wait_ge(tsm, need)
                v.tensor_mul(lstg[0:8, 0:512], SA[0:8, 512 * th:512 * (th + 1)],
                             r2B[0:8, 512 * th:512 * (th + 1)])
                v.tensor_scalar(lstg[0:8, 0:512], lstg[0:8, 0:512], bR[0:8, 0:1], None, ALU.add) \
                    .then_inc(sm, 1)
            VE(_lev)
            C["ffb"].n += 1
        def _ltr(t, sm=C["ffa"].sem, bsm=C["ffb"].sem, b0=b0_lev):
            for th in range(2):
                t.wait_ge(bsm, b0 + th + 1)
                for j in range(4):
                    ins = t.transpose(PU[:, 8 * (4 * th + j):8 * (4 * th + j + 1)],
                                      lstg[0:8, 128 * j:128 * (j + 1)], idf[0:8, 0:8])
                ins.then_inc(sm, 1)
        TE(_ltr)
        C["ffa"].n += 2
        W(VE, C["ffa"], C["ffa"].n)
        C["ev"].n += 1
        VE(lambda v, sm=C["ev"].sem: (v.tensor_copy(ltok[:, :], PU[:, 0:64]),
                                      v.drain().then_inc(sm, 1))[-1])
        n_ev_ltok = C["ev"].n
        W(VE, C["ev"], n_ev_ltok)
        def _gate(v, sm=C["ffb"].sem):
            for tt in range(8):
                v.tensor_reduce(lw[:, tt:tt + 1], ltok[:, 8 * tt:8 * (tt + 1)], AX.X, ALU.max)
            v.drain()
            for tt in range(8):
                v.tensor_scalar(ltok[:, 8 * tt:8 * (tt + 1)], ltok[:, 8 * tt:8 * (tt + 1)],
                                lw[:, tt:tt + 1], None, ALU.subtract)
            v.drain().then_inc(sm, 1)
        VE(_gate)
        C["ffb"].n += 1
        W(SE, C["ffb"], C["ffb"].n)
        C["ffc"].n += 1
        SE(lambda s, sm=C["ffc"].sem: s.activation(ltok[:, :], ltok[:, :], AF.Exp,
                                                   bias=c_zero[:, 0:1], scale=c_one[:, 0:1])
           .then_inc(sm, 1))
        W(VE, C["ffc"], C["ffc"].n)
        def _top2(v, sm=C["ffb"].sem):
            for tt in range(8):
                v.tensor_reduce(lw[:, 8 + tt:9 + tt], ltok[:, 8 * tt:8 * (tt + 1)], AX.X, ALU.add)
            v.drain()
            v.reciprocal(lw[:, 8:16], lw[:, 8:16])
            v.drain()
            for tt in range(8):
                v.tensor_scalar(ltok[:, 8 * tt:8 * (tt + 1)], ltok[:, 8 * tt:8 * (tt + 1)],
                                lw[:, 8 + tt:9 + tt], None, ALU.mult)
            v.drain()
            for tt in range(8):
                v.tensor_reduce(lw[:, 16 + tt:17 + tt], ltok[:, 8 * tt:8 * (tt + 1)], AX.X, ALU.max)
            v.drain()
            for tt in range(8):
                v.tensor_scalar(msk[:, 8 * tt:8 * (tt + 1)], ltok[:, 8 * tt:8 * (tt + 1)],
                                lw[:, 16 + tt:17 + tt], None, ALU.is_equal)
            v.drain()
            for tt in range(8):
                v.tensor_scalar(msk[:, 8 * tt:8 * (tt + 1)], msk[:, 8 * tt:8 * (tt + 1)],
                                -1e9, None, ALU.mult)
            v.drain()
            v.tensor_add(msk[:, 0:64], msk[:, 0:64], ltok[:, :])
            v.drain()
            for tt in range(8):
                v.tensor_reduce(lw[:, 24 + tt:25 + tt], msk[:, 8 * tt:8 * (tt + 1)], AX.X, ALU.max)
            v.drain()
            for tt in range(8):
                v.tensor_scalar(msk[:, 8 * tt:8 * (tt + 1)], ltok[:, 8 * tt:8 * (tt + 1)],
                                lw[:, 24 + tt:25 + tt], None, ALU.is_ge)
            v.drain()
            v.tensor_mul(ltok[:, :], ltok[:, :], msk[:, 0:64])
            v.drain().then_inc(sm, 1)
        VE(_top2)
        C["ffb"].n += 1
        n_ffb_gate = C["ffb"].n
        if dev:
            W(YE, C["ffb"], n_ffb_gate)
            C["dbgs"].n += 16
            YE(lambda y, sm=C["dbgs"].sem: y.dma_start(out=dbg["gate"], in_=ltok[:, :]).then_inc(sm, 16))
        W(TE, C["ffb"], n_ffb_gate)
        def _gtr(t, sm=C["ffa"].sem, esm=C["ev"].sem, eb=C["ev"].n):
            for b in range(2):
                if b == 1:
                    t.wait_ge(esm, eb + 1)
                for j in range(4):
                    ins = t.transpose(PU[0:8, 128 * j:128 * (j + 1)],
                                      ltok[:, 8 * (4 * b + j):8 * (4 * b + j + 1)], idf[:, :])
                ins.then_inc(sm, 1)
        TE(_gtr)
        C["ffa"].n += 2
        for b in range(2):
            W(VE, C["ffa"], C["ffa"].n - 1 + b)
            C["ev"].n += 1
            VE(lambda v, b=b, sm=C["ev"].sem: (
                v.tensor_copy(gateT[0:8, 512 * b:512 * (b + 1)], PU[0:8, 0:512]),
                v.drain().then_inc(sm, 1))[-1])
        n_ev_gT = C["ev"].n

        W(VE, C["ffa"], n_ffa_mm)
        def _hev(v, sm=C["ffc"].sem):
            v.tensor_mul(het[0], SB2[:, 0:512], r2B[:, 0:512])
            v.tensor_mul(het[1], SB2[:, 512:1024], r2B[:, 512:1024]).then_inc(sm, 1)
        VE(_hev)
        C["ffc"].n += 1
        W(SE, C["ffc"], C["ffc"].n)
        def _silu(s, sm=C["ffc"].sem):
            s.activation(a_sb[:, 0:512], het[0], AF.Silu, bias=bD[:, 0:1], scale=c_one[:, 0:1])
            s.activation(a_sb[:, 512:1024], het[1], AF.Silu, bias=bD[:, 0:1], scale=c_one[:, 0:1]) \
                .then_inc(sm, 1)
        SE(_silu)
        C["ffc"].n += 1
        n_ffc_silu = C["ffc"].n
        W(TE, C["ev"], n_ev_gT)
        def _gx(t, sm=C["ffa"].sem):
            t.matmul(SA[:, 0:512], emat[0:8, :], gateT[0:8, 0:512], start=True, stop=True)
            t.matmul(SA[:, 512:1024], emat[0:8, :], gateT[0:8, 512:1024], start=True, stop=True) \
                .then_inc(sm, 1)
        TE(_gx)
        C["ffa"].n += 1
        W(VE, C["ffa"], C["ffa"].n)
        W(VE, C["ffc"], n_ffc_silu)
        def _a2(v, sm=C["ffb"].sem):
            v.tensor_mul(a2_sb[:, 0:512], a_sb[:, 0:512], SA[:, 0:512])
            v.tensor_mul(a2_sb[:, 512:1024], a_sb[:, 512:1024], SA[:, 512:1024]).then_inc(sm, 1)
        VE(_a2)
        C["ffb"].n += 1
        n_ffb_a2 = C["ffb"].n

        up_slots = [SA[:, 0:512], SA[:, 512:1024], SA[:, 1024:1536],
                    SB2[:, 0:512], SB2[:, 512:1024], SB2[:, 1024:1536]]
        def emit_x1r(dc):
            W(YE, C["x1s0"], C["x1s0"].n)
            W(YE, C["x1s1"], C["x1s1"].n)
            W(YE, C["rbe"], C["rbe"].n)
            if dc >= 2:
                W(YE, C["outp%d" % (dc % 2)], 16 * (dc // 2))
            C["x1rp%d" % (dc % 2)].n += 16
            YE(lambda y, dc=dc, sm=C["x1rp%d" % (dc % 2)].sem: y.dma_start(
                out=xch[dc % 2][:, :], in_=x1_dram[dc, :, :]).then_inc(sm, 16))
        emit_x1r(0)
        emit_x1r(1)
        W(TE, C["ffb"], n_ffb_a2)
        for dc in range(16):
            def _up(t, dc=dc, sm=C["ffd"].sem, esm=C["upev"].sem, eb=C["upev"].n - 2 * dc):
                for th in range(2):
                    k = 2 * dc + th
                    if k >= 6:
                        t.wait_ge(esm, k - 5)
                    ins = t.matmul(up_slots[k % 6], U_sb[:, 128 * dc:128 * (dc + 1)],
                                   a2_sb[:, 512 * th:512 * (th + 1)], start=True, stop=True)
                ins.then_inc(sm, 1)
            TE(_up)
            C["ffd"].n += 1
            W(VE, C["ffd"], C["ffd"].n)
            W(VE, C["x1rp%d" % (dc % 2)], 16 * (dc // 2 + 1))
            def _res2(v, dc=dc, esm=C["upev"].sem):
                for th in range(2):
                    sl = up_slots[(2 * dc + th) % 6]
                    v.scalar_tensor_tensor(xch[dc % 2][:, 512 * th:512 * (th + 1)], sl,
                                           gbp[:, dc:dc + 1],
                                           xch[dc % 2][:, 512 * th:512 * (th + 1)],
                                           ALU.mult, ALU.add).then_inc(esm, 1)
            VE(_res2)
            C["upev"].n += 2
            W(YE, C["upev"], C["upev"].n)
            C["outp%d" % (dc % 2)].n += 16
            YE(lambda y, dc=dc, sm=C["outp%d" % (dc % 2)].sem: y.dma_start(
                out=out_d[128 * dc:128 * (dc + 1), :], in_=xch[dc % 2][:, :]).then_inc(sm, 16))
            if dc + 2 < 16:
                emit_x1r(dc + 2)

        if dev:
            W(YE, C["x1s0"], C["x1s0"].n)
            C["dbgs"].n += 16
            YE(lambda y, sm=C["dbgs"].sem: y.dma_start(out=dbg["x1"], in_=x1_dram[0, :, :]).then_inc(sm, 16))
            W(GE, C["outp0"], C["outp0"].n)
            W(GE, C["outp1"], C["outp1"].n)
            def _dffn(g, sm=C["dbgs"].sem):
                g.dma_start(out=dbg["x1f"], in_=x1_dram[:, :, :]).then_inc(sm, 16)
                g.dma_start(out=dbg["a2"], in_=a2_sb[:, :]).then_inc(sm, 16)
                g.dma_start(out=dbg["het"], in_=SC[:, 3840:4864]).then_inc(sm, 16)
                g.dma_start(out=dbg["r2b"], in_=r2B[:, :]).then_inc(sm, 16)
                g.dma_start(out=dbg["gbp"], in_=gbp[:, :]).then_inc(sm, 16)
                g.dma_start(out=dbg["usb"], in_=U_sb[:, :]).then_inc(sm, 16)
            GE(_dffn)
            C["dbgs"].n += 96

        W(YE, C["outp0"], C["outp0"].n)
        W(YE, C["outp1"], C["outp1"].n)
        if dev:
            W(YE, C["dbgs"], C["dbgs"].n)

        block = es.enter_context(nc.Block())

        @block.tensor
        def _(t):
            for f in prog["t"]:
                f(t)

        @block.vector
        def _(v):
            for f in prog["v"]:
                f(v)

        @block.scalar
        def _(s):
            for f in prog["s"]:
                f(s)

        @block.gpsimd
        def _(g):
            for f in prog["g"]:
                f(g)

        @block.sync
        def _(y):
            for f in prog["y"]:
                f(y)

    return nc


def _prep_inputs(inputs):
    x = np.asarray(inputs["L0_attn_hidden_states"])
    Wqkv = np.asarray(inputs["L0_attn_Wqkv_w"])
    bqkv = np.asarray(inputs["L0_attn_Wqkv_b"])
    ow = np.asarray(inputs["L0_attn_out_w"])
    outb = np.asarray(inputs["L0_attn_out_b"])
    rw = np.asarray(inputs["L0_ffn_router_w"])
    dw = np.asarray(inputs["L0_ffn_down_w"])
    uw = np.asarray(inputs["L0_ffn_up_w"])
    cv = np.asarray(inputs["L0_c"])
    wm = np.asarray(inputs["L0_wmod"])

    xT = np.ascontiguousarray(x[0].T)
    owT = np.ascontiguousarray(ow.T).reshape(16, 128, D)
    rwT = np.ascontiguousarray(rw.T).reshape(16, 128, 8)
    wdT = np.ascontiguousarray(dw.reshape(128, D).T).reshape(16, 128, 128)
    wuT = np.ascontiguousarray(np.transpose(uw, (0, 2, 1)).reshape(128, D))
    cvec = np.ascontiguousarray(cv.reshape(16, 128).T)
    outbv = np.ascontiguousarray(outb.reshape(16, 128).T)
    idbf = np.eye(128, dtype=ml_dtypes.bfloat16)
    idf = np.eye(128, dtype=np.float32)
    emat = np.zeros((8, 128), dtype=ml_dtypes.bfloat16)
    for e in range(8):
        emat[e, 16 * e:16 * (e + 1)] = 1.0

    in_maps = []
    for i in range(NCORE):
        qs = Wqkv[256 * i:256 * (i + 1), :]
        ks = Wqkv[D + 256 * i:D + 256 * (i + 1), :]
        vs = Wqkv[2 * D + 256 * i:2 * D + 256 * (i + 1), :]
        wqi = np.ascontiguousarray(np.concatenate([qs, ks, vs], axis=0).T)
        bqi = np.concatenate([bqkv[256 * i:256 * (i + 1)],
                              bqkv[D + 256 * i:D + 256 * (i + 1)],
                              bqkv[2 * D + 256 * i:2 * D + 256 * (i + 1)]])
        bqi = np.ascontiguousarray(bqi.reshape(6, 128).T)
        wms = wm[:, 1536 * i:1536 * (i + 1)]
        wmt = np.ascontiguousarray(wms.reshape(16, 128, 12, 128).transpose(0, 2, 1, 3))
        in_maps.append({
            "xfull": xT,
            "xsh": np.ascontiguousarray(xT[:, TOK * i:TOK * (i + 1)]),
            "wq": wqi, "bq": bqi, "wmod": wmt, "cvec": cvec,
            "ow": owT, "outb": outbv, "rw": rwT, "wd": wdT, "wu": wuT,
            "idbf": idbf, "idf": idf, "emat": emat,
        })
    return in_maps


_CACHE = {}


def kernel(**inputs):
    in_maps = _prep_inputs(inputs)
    if "nc" not in _CACHE:
        _CACHE["nc"] = build_nc(dev=DEV)
    res = run_bass_kernel_spmd(_CACHE["nc"], in_maps, core_ids=list(range(NCORE)))
    out = np.empty((S, D), dtype=np.float32)
    for i in range(NCORE):
        out[TOK * i:TOK * (i + 1), :] = res.results[i]["out"].T
    if DEV:
        kernel.last_res = res
    return out.reshape(1, S, D)
